# version 06
# baseline (speedup 1.0000x reference)
import math
import numpy as np
import ml_dtypes
import concourse.bass as bass
import concourse.mybir as mybir
from concourse.bass_utils import run_bass_kernel_spmd

F32 = mybir.dt.float32
BF16 = mybir.dt.bfloat16
U32 = mybir.dt.uint32
I32 = mybir.dt.int32
AF = mybir.ActivationFunctionType
ALU = mybir.AluOpType
AX = mybir.AxisListType.X

P = 128
TOK = 1024
D = 4096
KT = D // P
NT = TOK // P
NCORES = 8
NW = 512

DIN = 8192
G = 8
NSTATE = 128
CONV_DIM = DIN + 2 * G * NSTATE
NXB = CONV_DIM // P
NH = 128
HD = 64
EPS = 1e-5
NSL_GATE = 16
NSL_XBC = 20
NSL_IP = 36
NEG_BIG = -30000.0

NDMA = 6


class Sched:
    STREAMS = ("pe", "dve", "act", "pool", "sync")

    def __init__(self):
        self.ops = {s: [] for s in self.STREAMS}
        self.ncc = 0
        self.dma_count = {s: 0 for s in self.STREAMS}
        self.last_dma = {s: [None] * NDMA for s in self.STREAMS}

    def op(self, stream, emit, after=()):
        lst = self.ops[stream]
        tok = (stream, len(lst))
        lst.append({"emit": emit, "after": [a for a in after if a is not None],
                    "kind": "op"})
        return tok

    def dma(self, stream, emit, after=()):
        lst = self.ops[stream]
        tok = (stream, len(lst))
        lst.append({"emit": emit, "after": [a for a in after if a is not None],
                    "kind": "dma"})
        slot = self.dma_count[stream] % NDMA
        self.dma_count[stream] += 1
        self.last_dma[stream][slot] = tok
        return tok

    def cc(self, emit, after=()):
        lst = self.ops["pool"]
        tok = ("pool", len(lst))
        lst.append({"emit": emit, "after": [a for a in after if a is not None],
                    "kind": "cc", "ccid": self.ncc})
        self.ncc += 1
        return tok

    def frontier(self):
        out = []
        for s in self.STREAMS:
            if not self.ops[s]:
                continue
            out.append((s, len(self.ops[s]) - 1))
            for tok in self.last_dma[s]:
                if tok is not None:
                    out.append(tok)
        return list(dict.fromkeys(out))

    def finalize(self, nc, block, ctx_stack):
        waited = set()
        for s in self.STREAMS:
            for o in self.ops[s]:
                waited.update(o["after"])
        sem_objs = {}

        def mk_sem(name):
            g = nc.semaphore(name)
            h = g.__enter__()
            ctx_stack.append((name, g))
            return h

        stream_sem = {s: mk_sem("q_" + s) for s in self.STREAMS}
        dma_sems = {s: [mk_sem(f"d_{s}{i}") for i in range(NDMA)]
                    for s in ("act", "pool", "sync", "dve", "pe")}
        cc_sems = [mk_sem(f"cc{i}") for i in range(self.ncc)]

        tokinfo = {}
        for s in self.STREAMS:
            cnt = 0
            dma_i = 0
            dma_seq = [0] * NDMA
            for idx, o in enumerate(self.ops[s]):
                tok = (s, idx)
                if o["kind"] == "dma":
                    slot = dma_i % NDMA
                    dma_seq[slot] += 1
                    o["sem"] = dma_sems[s][slot]
                    o["val"] = 16 * dma_seq[slot]
                    o["inc"] = 16
                    if dma_seq[slot] > 1:
                        o["after"].append(("_dmaslot", s, slot, 16 * (dma_seq[slot] - 1)))
                    dma_i += 1
                    tokinfo[tok] = (o["sem"], o["val"])
                elif o["kind"] == "cc":
                    o["sem"] = cc_sems[o["ccid"]]
                    o["val"] = 1
                    o["inc"] = 1
                    tokinfo[tok] = (o["sem"], 1)
                else:
                    if tok in waited:
                        cnt += 1
                        o["sem"] = stream_sem[s]
                        o["val"] = cnt
                        o["inc"] = 1
                        tokinfo[tok] = (o["sem"], cnt)
                    else:
                        o["sem"] = None
        engmap = {"pe": block.tensor, "dve": block.vector, "act": block.scalar,
                  "pool": block.gpsimd, "sync": block.sync}
        for s in self.STREAMS:
            ops = self.ops[s]
            if not ops:
                continue

            def body(eng, s=s, ops=ops):
                known = {}
                for o in ops:
                    for a in o["after"]:
                        if isinstance(a, tuple) and a and a[0] == "_dmaslot":
                            h, v = dma_sems[a[1]][a[2]], a[3]
                        else:
                            h, v = tokinfo[a]
                        if known.get(id(h), -1) < v:
                            eng.wait_ge(h, v)
                            known[id(h)] = v
                    inst = o["emit"](eng)
                    if o["sem"] is not None:
                        inst.then_inc(o["sem"], o["inc"])
            engmap[s](body)


def _pack_a(WT, nsl, nw=NW):
    K = WT.shape[0]
    kt = K // P
    return np.ascontiguousarray(
        WT.reshape(kt, P, nsl, nw).transpose(2, 1, 0, 3).reshape(nsl, P, kt * nw))


def _pack_b(WT, nm):
    K = WT.shape[0]
    kt = K // P
    return np.ascontiguousarray(
        WT.reshape(kt, P, nm, P).transpose(2, 1, 0, 3).reshape(nm, P, kt * P))



def build(seq):
    nc = bass.Bass("TRN2", target_bir_lowering=False, debug=False)
    sc = Sched()

    x_ext = nc.dram_tensor("x", [TOK, D], F32, kind="ExternalInput").ap()
    out_ext = nc.dram_tensor("out", [TOK, D], F32, kind="ExternalOutput").ap()
    ident_in = nc.dram_tensor("ident", [P, P], F32, kind="ExternalInput").ap()
    ltones_in = nc.dram_tensor("ltones", [P, P], F32, kind="ExternalInput").ap()
    strl_in = nc.dram_tensor("strl", [P, P], F32, kind="ExternalInput").ap()
    maskneg_in = nc.dram_tensor("maskneg", [P, P], F32, kind="ExternalInput").ap()
    expand2_in = nc.dram_tensor("expand2", [8, P], F32, kind="ExternalInput").ap()
    iota8_in = nc.dram_tensor("iota8", [15, 8], F32, kind="ExternalInput").ap()
    emap_in = nc.dram_tensor("emap", [15, 1], I32, kind="ExternalInput").ap()
    negid_in = nc.dram_tensor("negid16", [16, 2048], F32, kind="ExternalInput").ap()

    need_mamba = "mamba" in seq
    need_ffn = ("ffn" in seq) or ("ffn2" in seq)
    need_attn = "attn" in seq

    w_ip = nc.dram_tensor("w_ip", [NSL_IP, P, KT * NW], F32, kind="ExternalInput").ap()
    w_dt = nc.dram_tensor("w_dt", [P, KT * P], F32, kind="ExternalInput").ap()
    w_ob = nc.dram_tensor("w_ob", [32, P, 64 * P], F32, kind="ExternalInput").ap()
    w_base = nc.dram_tensor("w_base", [8, P, KT * NW], F32, kind="ExternalInput").ap()
    w_ra = nc.dram_tensor("w_ra", [P, KT * 143], F32, kind="ExternalInput").ap()
    w_lb = nc.dram_tensor("w_lb", [P, D], F32, kind="ExternalInput").ap()
    w_qkv = nc.dram_tensor("w_qkv", [6, P, KT * NW], F32, kind="ExternalInput").ap()
    w_o2 = nc.dram_tensor("w_o2", [8, P, 8 * NW], F32, kind="ExternalInput").ap()
    convw_in = nc.dram_tensor("convw", [P, NXB * 4], F32, kind="ExternalInput").ap()
    convb_in = nc.dram_tensor("convb", [P, NXB], F32, kind="ExternalInput").ap()
    alog_in = nc.dram_tensor("alog", [1, NH], F32, kind="ExternalInput").ap()
    dvec_in = nc.dram_tensor("dvec", [1, NH], F32, kind="ExternalInput").ap()
    dtb_in = nc.dram_tensor("dtb", [1, NH], F32, kind="ExternalInput").ap()
    normw_in = nc.dram_tensor("normw", [P, 64], F32, kind="ExternalInput").ap()
    baseb_in = nc.dram_tensor("baseb", [1, D], F32, kind="ExternalInput").ap()
    rb_in = nc.dram_tensor("rb", [1, 15], F32, kind="ExternalInput").ap()
    bqkv_in = nc.dram_tensor("bqkv", [1, 3072], F32, kind="ExternalInput").ap()
    bo_in = nc.dram_tensor("bo", [1, D], F32, kind="ExternalInput").ap()
    halo_idx_in = nc.dram_tensor("halo_idx", [3, 1], U32, kind="ExternalInput").ap()
    halo_sel_in = nc.dram_tensor("halo_sel", [3, 1], F32, kind="ExternalInput").ap()
    st_idx_in = nc.dram_tensor("st_idx", [P, 1], U32, kind="ExternalInput").ap()
    st_sel_in = nc.dram_tensor("st_sel", [P, 1], F32, kind="ExternalInput").ap()
    kv_idx_in = nc.dram_tensor("kv_idx", [P, 32], U32, kind="ExternalInput").ap()

    dbg = {}
    if _DBG:
        dbg["xnT"] = nc.dram_tensor("dbg_xnT", [P, KT * TOK], BF16, kind="ExternalOutput").ap()
        dbg["probs"] = nc.dram_tensor("dbg_probs", [P, 16 * NT], F32, kind="ExternalOutput").ap()
        dbg["hw"] = nc.dram_tensor("dbg_hw", [P, NT * P], BF16, kind="ExternalOutput").ap()
        dbg["wTs"] = nc.dram_tensor("dbg_wTs", [8, NT * P], BF16, kind="ExternalOutput").ap()
        dbg["cnt"] = nc.dram_tensor("dbg_cnt", [P, 16 * NT], F32, kind="ExternalOutput").ap()
        dbg["wp16"] = nc.dram_tensor("dbg_wp16", [P, 16 * NT], BF16, kind="ExternalOutput").ap()
        dbg["onehot"] = nc.dram_tensor("dbg_onehot", [15, 8], BF16, kind="ExternalOutput").ap()
        dbg["wpT"] = nc.dram_tensor("dbg_wpT", [15, NT * P], BF16, kind="ExternalOutput").ap()
    n_bound = len(seq) - 1
    streams_d = [nc.dram_tensor(f"s{i+1}", [NT, 8, P, NW], F32).ap() for i in range(n_bound)]
    if need_mamba:
        z_dram = nc.dram_tensor("z_dram", [NSL_IP, TOK, NW], BF16).ap()
        xt_dram = nc.dram_tensor("xt_dram", [NT, P, DIN], BF16).ap()
        snap_dram = nc.dram_tensor("snap_dram", [NT, P, DIN], BF16).ap()
        yt_dram = nc.dram_tensor("yt_dram", [NT, 64, P, P], BF16).ap()
        acst_dram = nc.dram_tensor("acst_dram", [4, P, NT * P], BF16).ap()
        cc_halo_in = nc.dram_tensor("cc_halo_in", [3, CONV_DIM], BF16).ap()
        cc_halo_out = nc.dram_tensor("cc_halo_out", [NCORES * 3, CONV_DIM], BF16).ap()
        cc_st_in = nc.dram_tensor("cc_st_in", [P, DIN], F32).ap()
        cc_st_out = nc.dram_tensor("cc_st_out", [NCORES * P, DIN], F32).ap()
    if need_attn:
        cc_kv_in = nc.dram_tensor("cc_kv_in", [2048, 1024], BF16).ap()
        cc_kv_out = nc.dram_tensor("cc_kv_out", [NCORES * 2048, 1024], BF16).ap()

    ctx_stack = []
    _uid = [0]

    def sb(name, shape, dtype):
        _uid[0] += 1
        name = f"{name}_u{_uid[0]}"
        g = nc.sbuf_tensor(name, shape, dtype)
        t = g.__enter__()
        ctx_stack.append((name, g))
        return t

    def ps(name, shape, dtype):
        _uid[0] += 1
        name = f"{name}_u{_uid[0]}"
        g = nc.psum_tensor(name, shape, dtype)
        t = g.__enter__()
        ctx_stack.append((name, g))
        return t

    def mark():
        return len(ctx_stack)

    def pop_to(m):
        while len(ctx_stack) > m:
            ctx_stack.pop()[1].__exit__(None, None, None)

    idn = sb("idn", [P, P], BF16)
    idnf = sb("idnf", [P, P], F32)
    ones_row = sb("ones_row", [1, P], BF16)
    mones_row = sb("mones_row", [1, P], BF16)
    cvals = sb("cvals", [P, 2], F32)

    t_idn = sc.dma("pool", lambda g: g.dma_start(out=idn[:], in_=ident_in))
    t_idnf = sc.dma("sync", lambda g: g.dma_start(out=idnf[:], in_=ident_in))
    t_ones = sc.op("pool", lambda g: g.memset(ones_row[:], 1.0))
    t_mones = sc.op("pool", lambda g: g.memset(mones_row[:], -1.0))
    t_c0 = sc.op("pool", lambda g: g.memset(cvals[:, 0:1], EPS))
    t_c1 = sc.op("pool", lambda g: g.memset(cvals[:, 1:2], 1.0))
    nc.const_aps.register(F32, EPS, cvals[:, 0:1])
    nc.const_aps.register(F32, 1.0, cvals[:, 1:2])

    def emit_ln(src_dram, src_rowmajor, ln_bufs, xnT, deps):
        xtile, xnt_b, cols, ptr = ln_bufs
        t_norm = [None] * NT
        tr_last = [None] * NT
        t_evac = {}
        for t in range(NT):
            af = list(deps) + ([t_norm[t - 2]] if t >= 2 else [])
            if src_rowmajor:
                t_ld = sc.dma("sync", lambda g, t=t: g.dma_start(
                    out=xtile[t % 2][:], in_=src_dram[t * P:(t + 1) * P, :]), after=af)
            else:
                t_ld = sc.dma("sync", lambda g, t=t: g.dma_start(
                    out=xtile[t % 2][:].rearrange("p (a b) -> p a b", a=8),
                    in_=src_dram[t].rearrange("a p b -> p a b")), after=af)
            xt = xtile[t % 2]
            co = cols[:, t * 8:(t + 1) * 8]
            a2 = sc.op("act", lambda s, t=t, xt=xt, co=co: s.activation(
                xnt_b[t % 2][:], xt[:], AF.Square, accum_out=co[:, 1:2]),
                after=[t_ld] + ([tr_last[t - 2]] if t >= 2 else []))
            a1 = sc.op("dve", lambda v, xt=xt, co=co: v.tensor_reduce(
                co[:, 0:1], xt[:], AX, ALU.add), after=[t_ld])
            a3 = sc.op("dve", lambda v, co=co: v.tensor_scalar(
                co[:, 2:4], co[:, 0:2], 1.0 / D, None, ALU.mult), after=[a1, a2])
            a4 = sc.op("dve", lambda v, co=co: v.tensor_tensor(
                co[:, 4:5], co[:, 2:3], co[:, 2:3], ALU.mult), after=[a3])
            a5 = sc.op("dve", lambda v, co=co: v.tensor_tensor(
                co[:, 5:6], co[:, 3:4], co[:, 4:5], ALU.subtract), after=[a4])
            a6 = sc.op("act", lambda s, co=co: s.activation(
                co[:, 6:7], co[:, 5:6], AF.Ln, bias=EPS, scale=1.0), after=[a5])
            a7 = sc.op("act", lambda s, co=co: s.activation(
                co[:, 7:8], co[:, 6:7], AF.Exp, scale=-0.5), after=[a6])
            afn = [a7, a2] + ([tr_last[t - 2]] if t >= 2 else [])
            t_norm[t] = sc.op("dve", lambda v, t=t, xt=xt, co=co: v.tensor_scalar(
                xnt_b[t % 2][:], xt[:], co[:, 2:3], co[:, 7:8],
                ALU.subtract, ALU.mult), after=afn)
            for kt in range(KT):
                i = t * KT + kt
                af2 = [t_norm[t], t_idn]
                if i >= 4:
                    af2.append(t_evac[i - 4])
                tr = sc.op("pe", lambda t_, t=t, kt=kt, i=i: t_.transpose(
                    ptr[i % 4][:], xnt_b[t % 2][:, kt * P:(kt + 1) * P], idn[:]), after=af2)
                t_evac[i] = sc.op("act", lambda s, t=t, kt=kt, i=i: s.copy(
                    xnT[:, kt * TOK + t * P:kt * TOK + (t + 1) * P], ptr[i % 4][:]), after=[tr])
            tr_last[t] = t_evac[t * KT + KT - 1]
        all_done = [t_evac[i] for i in range(NT * KT)]
        return all_done, t_norm

    def mk_ln_bufs(pfx):
        xtile = [sb(f"{pfx}_x{i}", [P, D], F32) for i in range(2)]
        xnt_b = [sb(f"{pfx}_n{i}", [P, D], BF16) for i in range(2)]
        cols = sb(f"{pfx}_cols", [P, 8 * NT], F32)
        ptr = [ps(f"{pfx}_tr{i}", [P, P], BF16) for i in range(4)]
        return (xtile, xnt_b, cols, ptr)


    def emit_ffn(src_dram, src_rm, dst_dram, dst_rm, deps):
        m0 = mark()
        xnT = sb("xnT_f", [P, KT * TOK], BF16)
        mL = mark()
        ln_bufs = mk_ln_bufs("fln")
        xn_done, t_norm = emit_ln(src_dram, src_rm, ln_bufs, xnT, deps)
        pop_to(mL)
        deps = list(deps) + [xn_done[-1], t_norm[-1]]

        wra = sb("wra", [P, KT * 143], BF16)
        wlb = sb("wlb", [P, D], BF16)
        bb_row = sb("bb_row", [1, D], BF16)
        rb_row = sb("rb_row", [1, 15], BF16)
        emap_i = sb("emap_i", [15, 1], I32)
        emap_f = sb("emap_f", [15, 1], F32)
        iota8 = sb("sb_iota8", [15, 8], F32)
        onehot = sb("onehot", [15, 8], BF16)
        expand2 = sb("sb_expand2", [8, P], BF16)
        hw_all = sb("hw_all", [P, NT * P], BF16)
        hbf = sb("hbf", [P, NT * P], BF16)
        pbuf = sb("pbuf", [P, 16 * NT], F32)
        cnt = sb("cnt", [P, 16 * NT], F32)
        wp16 = sb("wp16", [P, 16 * NT], BF16)
        wpT = sb("wpT", [15, NT * P], BF16)
        wTs = sb("wTs", [8, NT * P], BF16)
        scrf = sb("scrf", [P, 4 * NT], F32)
        wbuf = [sb(f"fw{i}", [P, 16 * NW], BF16) for i in range(3)]
        stage = [sb(f"fs{i}", [P, NW], F32) for i in range(4)]
        resb = [sb(f"fr{i}", [P, NW], F32) for i in range(4)]
        pra = [ps(f"fpra{i}", [P, 160], F32) for i in range(2)]
        ptr2 = ps("fptr2", [P, P], BF16)
        pw2 = ps("fpw2", [P, P], F32)
        pmm = [ps(f"fpm{i}", [P, NW], F32) for i in range(4)]

        t_wra = sc.dma("pool", lambda g: g.dma_start(out=wra[:], in_=w_ra), after=deps)
        t_wlb = sc.dma("pool", lambda g: g.dma_start(out=wlb[:], in_=w_lb), after=deps)
        t_bb = sc.dma("pool", lambda g: g.dma_start(out=bb_row[:], in_=baseb_in), after=deps)
        t_rb = sc.dma("pool", lambda g: g.dma_start(out=rb_row[:], in_=rb_in), after=deps)
        t_em = sc.dma("sync", lambda g: g.dma_start(out=emap_i[:], in_=emap_in), after=deps)
        t_io = sc.dma("sync", lambda g: g.dma_start(out=iota8[:], in_=iota8_in), after=deps)
        t_e2 = sc.dma("pool", lambda g: g.dma_start(out=expand2[:], in_=expand2_in), after=deps)
        t_pz = sc.op("pool", lambda g: g.memset(pbuf[:], 0.0), after=deps)
        t_cz = sc.op("pool", lambda g: g.memset(cnt[:], 0.0), after=deps)
        t_wz = sc.op("pool", lambda g: g.memset(wp16[:], 0.0), after=deps)
        t_emf = sc.op("dve", lambda v: v.tensor_copy(emap_f[:], emap_i[:]), after=[t_em])
        t_oh = sc.op("dve", lambda v: v.tensor_scalar(
            onehot[:], iota8[:], emap_f[:, 0:1], None, ALU.is_equal), after=[t_emf, t_io])

        t_hw = [None] * NT
        prev_pra_use = [[], []]
        prev_ptr2_use = None
        prev_pw2_use = None
        for t in range(NT):
            af = [t_wra] + deps + prev_pra_use[t % 2]
            mm = None
            for kt in range(KT):
                mm = sc.op("pe", lambda t_, t=t, kt=kt: t_.matmul(
                    pra[t % 2][:, 0:143],
                    xnT[:, kt * TOK + t * P:kt * TOK + (t + 1) * P],
                    wra[:, kt * 143:(kt + 1) * 143],
                    start=(kt == 0), stop=False),
                    after=(af if kt == 0 else []))
            mmb = sc.op("pe", lambda t_, t=t: t_.matmul(
                pra[t % 2][:, P:P + 15], ones_row[0:1, :], rb_row[0:1, :],
                start=False, stop=True), after=[mm, t_rb, t_ones])
            co = scrf[:, 4 * t:4 * t + 4]
            pb = pbuf[:, 16 * t:16 * t + 15]
            cn = cnt[:, 16 * t:16 * t + 15]
            wp = wp16[:, 16 * t:16 * t + 15]
            s1 = sc.op("dve", lambda v, t=t, co=co: v.reduce_max(
                co[:, 0:1], pra[t % 2][:, P:P + 15], AX, negate=True), after=[mmb])
            s2 = sc.op("act", lambda s, t=t, co=co, pb=pb: s.activation(
                pb, pra[t % 2][:, P:P + 15], AF.Exp,
                bias=co[:, 0:1], scale=1.0, accum_out=co[:, 1:2]), after=[s1, t_pz])
            s3 = sc.op("dve", lambda v, co=co: v.reciprocal(co[:, 2:3], co[:, 1:2]), after=[s2])
            s4 = sc.op("dve", lambda v, co=co, pb=pb: v.tensor_scalar(
                pb, pb, co[:, 2:3], None, ALU.mult), after=[s3])
            s5 = sc.op("dve", lambda v, pb=pb, cn=cn: v.tensor_scalar(
                cn, pb, pb[:, 0:1], None, ALU.is_lt), after=[s4, t_cz])
            for j in range(1, 15):
                s5 = sc.op("dve", lambda v, j=j, pb=pb, cn=cn: v.scalar_tensor_tensor(
                    cn, pb, pb[:, j:j + 1], cn, ALU.is_lt, ALU.add), after=[s5])
            s6 = sc.op("dve", lambda v, cn=cn: v.tensor_scalar(
                cn, cn, 7.5, None, ALU.is_lt), after=[s5])
            s7 = sc.op("dve", lambda v, pb=pb, cn=cn, wp=wp: v.tensor_tensor(
                wp, pb, cn, ALU.mult), after=[s6, t_wz])
            af8 = [s7, t_idn] + ([prev_ptr2_use] if prev_ptr2_use else [])
            s8 = sc.op("pe", lambda t_, wp=wp: t_.transpose(
                ptr2[0:15, 0:P], wp, idn[:]), after=af8)
            s8e = sc.op("act", lambda s, t=t: s.copy(
                wpT[:, t * P:(t + 1) * P], ptr2[0:15, 0:P]), after=[s8])
            af9 = [s8e, t_oh] + ([prev_pw2_use] if prev_pw2_use else [])
            s9 = sc.op("pe", lambda t_, t=t: t_.matmul(
                pw2[0:8, 0:P], onehot[:, 0:8], wpT[:, t * P:(t + 1) * P],
                start=True, stop=True), after=af9)
            s9e = sc.op("act", lambda s, t=t: s.copy(
                wTs[:, t * P:(t + 1) * P], pw2[0:8, 0:P]), after=[s9])
            s10 = sc.op("pe", lambda t_, t=t: t_.matmul(
                pw2[:, 0:P], wTs[0:8, t * P:(t + 1) * P], expand2[0:8, :],
                start=True, stop=True), after=[s9e, t_e2])
            s11 = sc.op("act", lambda s, t=t: s.copy(
                hbf[:, t * P:(t + 1) * P], pra[t % 2][:, 0:P]), after=[mmb])
            s12 = sc.op("dve", lambda v, t=t: v.tensor_tensor(
                hbf[:, t * P:(t + 1) * P], hbf[:, t * P:(t + 1) * P],
                pw2[:, 0:P], ALU.mult), after=[s10, s11])
            prev_pw2_use = s12
            prev_pra_use[t % 2] = [s2, s11]
            af13 = [s12, s8e]
            s13 = sc.op("pe", lambda t_, t=t: t_.transpose(
                ptr2[:, 0:P], hbf[:, t * P:(t + 1) * P], idn[:]), after=af13)
            prev_ptr2_use = sc.op("act", lambda s, t=t: s.copy(
                hw_all[:, t * P:(t + 1) * P], ptr2[:, 0:P]), after=[s13])
            t_hw[t] = prev_ptr2_use

        t_store = []
        buf_use = [None, None, None]
        prev_stage_use = {}
        for s_i in range(8):
            t_wh = [None, None]
            for hf in range(2):
                bidx = (2 * s_i + hf) % 3
                af = list(deps)
                if buf_use[bidx] is not None:
                    af.append(buf_use[bidx])
                t_wh[hf] = sc.dma("pool", lambda g, s_i=s_i, hf=hf, bidx=bidx: g.dma_start(
                    out=wbuf[bidx][:], in_=w_base[s_i][:, hf * 16 * NW:(hf + 1) * 16 * NW]),
                    after=af)
            mmlast = None
            for t in range(NT):
                gi = s_i * NT + t
                pb = pmm[gi % 4]
                af2 = [t_wh[0]] + deps
                if gi >= 4:
                    af2.append(prev_stage_use[gi % 4])
                for kt in range(KT):
                    hf = kt // 16
                    bidx = (2 * s_i + hf) % 3
                    mmlast = sc.op("pe", lambda t_, s_i=s_i, t=t, kt=kt, pb=pb, bidx=bidx: t_.matmul(
                        pb[:], xnT[:, kt * TOK + t * P:kt * TOK + (t + 1) * P],
                        wbuf[bidx][:, (kt % 16) * NW:(kt % 16 + 1) * NW],
                        start=(kt == 0), stop=False),
                        after=(af2 if kt == 0 else ([t_wh[1]] if kt == 16 else [])))
                    if t == NT - 1 and kt % 16 == 15:
                        buf_use[bidx] = mmlast
                mb = sc.op("pe", lambda t_, s_i=s_i, pb=pb: t_.matmul(
                    pb[:], ones_row[0:1, :], bb_row[0:1, s_i * NW:(s_i + 1) * NW],
                    start=False, stop=False), after=[mmlast, t_bb])
                ml = sc.op("pe", lambda t_, s_i=s_i, t=t, pb=pb: t_.matmul(
                    pb[:], hw_all[:, t * P:(t + 1) * P], wlb[:, s_i * NW:(s_i + 1) * NW],
                    start=False, stop=True), after=[mb, t_hw[t], t_wlb])
                rl = sc.dma("sync", lambda g, s_i=s_i, t=t, gi=gi: g.dma_start(
                    out=resb[gi % 4][:],
                    in_=(src_dram[t * P:(t + 1) * P, s_i * NW:(s_i + 1) * NW] if src_rm
                         else src_dram[t, s_i])),
                    after=(deps if gi < 4 else [prev_stage_use[gi % 4]]))
                ev = sc.op("dve", lambda v, gi=gi, pb=pb: v.tensor_tensor(
                    stage[gi % 4][:], pb[:], resb[gi % 4][:], ALU.add), after=[ml, rl])
                st = sc.dma("sync", lambda g, s_i=s_i, t=t, gi=gi: g.dma_start(
                    out=(dst_dram[t * P:(t + 1) * P, s_i * NW:(s_i + 1) * NW] if dst_rm
                         else dst_dram[t, s_i]), in_=stage[gi % 4][:]), after=[ev])
                prev_stage_use[gi % 4] = st
                t_store.append(st)
        if _DBG:
            sc.dma("sync", lambda g: g.dma_start(out=dbg["xnT"], in_=xnT[:]), after=[xn_done[-1]])
            sc.dma("sync", lambda g: g.dma_start(out=dbg["probs"], in_=pbuf[:]), after=t_hw)
            sc.dma("sync", lambda g: g.dma_start(out=dbg["hw"], in_=hw_all[:]), after=t_hw)
            sc.dma("sync", lambda g: g.dma_start(out=dbg["wTs"], in_=wTs[:]), after=t_hw)
            sc.dma("sync", lambda g: g.dma_start(out=dbg["cnt"], in_=cnt[:]), after=t_hw)
            sc.dma("sync", lambda g: g.dma_start(out=dbg["wp16"], in_=wp16[:]), after=t_hw)
            sc.dma("sync", lambda g: g.dma_start(out=dbg["onehot"], in_=onehot[:]), after=t_hw)
            sc.dma("sync", lambda g: g.dma_start(out=dbg["wpT"], in_=wpT[:]), after=t_hw)
        fr = sc.frontier()
        pop_to(m0)
        return fr

    def emit_mamba(src_dram, src_rm, dst_dram, dst_rm, deps):
        m0 = mark()
        abc = sb("abc", [P, P], F32)
        abc8 = sb("abc8", [P, NT * P], F32)
        dbc = sb("dbc", [P, P], BF16)
        ltones = sb("ltones_b", [P, P], BF16)
        strl = sb("strl_b", [P, P], BF16)
        mask4 = sb("mask4", [P, 4 * P], BF16)
        convw = sb("convw_b", [P, NXB * 4], F32)
        convb = sb("convb_b", [P, NXB], F32)
        rows = sb("m_rows", [1, 2 * P], F32)
        rowsb = sb("m_rowsb", [1, 2 * P], BF16)
        t_lt = sc.dma("pool", lambda g: g.dma_start(out=ltones[:], in_=ltones_in), after=deps)
        t_sl = sc.dma("pool", lambda g: g.dma_start(out=strl[:], in_=strl_in), after=deps)
        t_m4 = sc.dma("pool", lambda g: g.dma_start(out=mask4[:, 0:P], in_=maskneg_in), after=deps)
        t_m4b = sc.op("dve", lambda v: v.tensor_copy(mask4[:, P:2 * P], mask4[:, 0:P]), after=[t_m4])
        t_m4c = sc.op("dve", lambda v: v.tensor_copy(mask4[:, 2 * P:4 * P], mask4[:, 0:2 * P]),
                      after=[t_m4b])
        t_cw = sc.dma("pool", lambda g: g.dma_start(out=convw[:], in_=convw_in), after=deps)
        t_cb = sc.dma("pool", lambda g: g.dma_start(out=convb[:], in_=convb_in), after=deps)
        t_al = sc.dma("sync", lambda g: g.dma_start(out=rows[0:1, 0:P], in_=alog_in), after=deps)
        t_dv = sc.dma("sync", lambda g: g.dma_start(out=rows[0:1, P:2 * P], in_=dvec_in), after=deps)
        a1 = sc.op("act", lambda s: s.activation(rows[0:1, 0:P], rows[0:1, 0:P], AF.Exp, scale=1.0),
                   after=[t_al])
        a2 = sc.op("dve", lambda v: v.tensor_scalar(rows[0:1, 0:P], rows[0:1, 0:P], -1.0, None,
                                                    ALU.mult), after=[a1])
        a3 = sc.op("act", lambda s: s.copy(rowsb[0:1, 0:P], rows[0:1, 0:P]), after=[a2])
        a4 = sc.op("act", lambda s: s.copy(rowsb[0:1, P:2 * P], rows[0:1, P:2 * P]), after=[t_dv])

        mS = mark()
        sdv = sb("sdv", [P, NT * P], F32)
        dsx = sb("dsx", [P, NT * P], F32)
        lam_all = sb("lam_all", [P, NT * P], F32)
        dtraw = lam_all
        nwc = sb("nwc", [P, 64], F32)
        BCT = sb("BCT", [P, NT * 2048], BF16)

        mpre = mark()
        pcon = ps("pcon", [P, NW], F32)
        t_nw = sc.dma("sync", lambda g: g.dma_start(out=nwc[:], in_=normw_in), after=deps)
        b1 = sc.op("pe", lambda t_: t_.matmul(pcon[:, 0:P], ones_row[0:1, :], rowsb[0:1, 0:P],
                                              start=True, stop=True), after=[a3, t_ones])
        b2 = sc.op("dve", lambda v: v.tensor_copy(abc[:], pcon[:, 0:P]), after=[b1])
        ab8 = []
        pb8 = b2
        for cc8 in range(NT):
            pb8 = sc.op("dve", lambda v, cc8=cc8: v.tensor_copy(
                abc8[:, cc8 * P:(cc8 + 1) * P], abc[:]), after=[pb8])
            ab8.append(pb8)
        b3 = sc.op("pe", lambda t_: t_.matmul(pcon[:, 0:P], ones_row[0:1, :], rowsb[0:1, P:2 * P],
                                              start=True, stop=True), after=[b2, a4])
        b4 = sc.op("dve", lambda v: v.tensor_copy(dbc[:], pcon[:, 0:P]), after=[b3])
        pop_to(mpre)
        deps_m = list(deps) + [t_m4c, t_cw, t_cb, b2, b4, t_nw]

        mX = mark()
        xnT = sb("xnT_m", [P, KT * TOK], BF16)
        mL = mark()
        ln_bufs = mk_ln_bufs("mln")
        xn_done, t_norm = emit_ln(src_dram, src_rm, ln_bufs, xnT, deps)
        pop_to(mL)
        depsA = list(deps) + [xn_done[-1], t_norm[-1]]
        wib = [sb(f"wib{i}", [P, 16 * NW], BF16) for i in range(3)]
        wdtb = sb("wdtb", [P, KT * P], BF16)
        dtb_row = sb("dtb_row", [1, NH], BF16)
        zstage = [sb(f"zst{i}", [P, NW], BF16) for i in range(4)]
        pz = [ps(f"pz{i}", [P, NW], F32) for i in range(4)]
        t_wdt = sc.dma("pool", lambda g: g.dma_start(out=wdtb[:], in_=w_dt), after=depsA)
        t_dtb = sc.dma("pool", lambda g: g.dma_start(out=dtb_row[:], in_=dtb_in), after=depsA)
        z_done = []
        buf_use = [None, None, None]
        prev_bank = {}
        for s_i in range(NSL_IP):
            t_wh = [None, None]
            for hf in range(2):
                bidx = (2 * s_i + hf) % 3
                af = list(depsA)
                if buf_use[bidx] is not None:
                    af.append(buf_use[bidx])
                t_wh[hf] = sc.dma("pool", lambda g, s_i=s_i, hf=hf, bidx=bidx: g.dma_start(
                    out=wib[bidx][:], in_=w_ip[s_i][:, hf * 16 * NW:(hf + 1) * 16 * NW]), after=af)
            for t in range(NT):
                gi = s_i * NT + t
                pb = pz[gi % 4]
                af2 = [t_wh[0]] + depsA
                if gi >= 4:
                    af2.append(prev_bank[gi % 4])
                mml = None
                for kt in range(KT):
                    hf = kt // 16
                    bidx = (2 * s_i + hf) % 3
                    mml = sc.op("pe", lambda t_, t=t, kt=kt, pb=pb, bidx=bidx: t_.matmul(
                        pb[:], xnT[:, kt * TOK + t * P:kt * TOK + (t + 1) * P],
                        wib[bidx][:, (kt % 16) * NW:(kt % 16 + 1) * NW],
                        start=(kt == 0), stop=(kt == KT - 1)),
                        after=(af2 if kt == 0 else ([t_wh[1]] if kt == 16 else [])))
                    if t == NT - 1 and kt % 16 == 15:
                        buf_use[bidx] = mml
                ev = sc.op("act", lambda s, gi=gi, pb=pb: s.copy(zstage[gi % 4][:], pb[:]),
                           after=[mml])
                st = sc.dma("sync", lambda g, s_i=s_i, t=t, gi=gi: g.dma_start(
                    out=z_dram[s_i][t * P:(t + 1) * P, :], in_=zstage[gi % 4][:]), after=[ev])
                prev_bank[gi % 4] = st
                z_done.append(st)
        dt_done = []
        for t in range(NT):
            pb = pz[t % 4]
            af2 = [t_wdt, prev_bank[t % 4]]
            mml = None
            for kt in range(KT):
                mml = sc.op("pe", lambda t_, t=t, kt=kt, pb=pb: t_.matmul(
                    pb[:, 0:P], xnT[:, kt * TOK + t * P:kt * TOK + (t + 1) * P],
                    wdtb[:, kt * P:(kt + 1) * P],
                    start=(kt == 0), stop=False), after=(af2 if kt == 0 else [])) 
            mb = sc.op("pe", lambda t_, pb=pb: t_.matmul(
                pb[:, 0:P], ones_row[0:1, :], dtb_row[0:1, :],
                start=False, stop=True), after=[mml, t_dtb, t_ones])
            dte = sc.op("dve", lambda v, t=t, pb=pb: v.tensor_copy(
                dtraw[:, t * P:(t + 1) * P], pb[:, 0:P]), after=[mb])
            prev_bank[t % 4] = dte
            dt_done.append(dte)
        t_hc = sc.dma("sync", lambda g: g.dma_start(
            out=cc_halo_in[:].rearrange("r (s w) -> r s w", s=NSL_XBC),
            in_=z_dram[NSL_GATE:NSL_IP, TOK - 3:TOK, :].rearrange("s r w -> r s w")),
            after=z_done[-1:] + z_done[::37])
        af_cc = [t_hc] + cc_chain
        t_cch = sc.cc(lambda g: g.collective_compute(
            "AllGather", ALU.bypass, replica_groups=[list(range(NCORES))],
            ins=[cc_halo_in[:]], outs=[cc_halo_out[:]]), after=af_cc)
        cc_chain.append(t_cch)
        m1_fin = [z_done[-1], dt_done[-1]]
        pop_to(mX)

        mA2 = mark()
        cum = sb("cum", [P, DIN], F32)
        xbct = sb("xbct", [P, NXB * 131], BF16)
        sidx = sb("sidx", [P, 1], U32)
        t_si = sc.dma("sync", lambda g: g.dma_start(out=sidx[:], in_=st_idx_in),
                      after=deps_m + m1_fin)
        xbct_v = xbct[:].rearrange("p (m w) -> p m w", w=131)

        mPP = mark()
        dtv = sb("dtv", [P, NT * P], F32)
        lnd = sb("lnd", [P, NT * P], F32)
        tmpa = sb("tmpa", [P, NT * P], F32)
        tmpb = sb("tmpb", [P, NT * P], F32)
        acs_all = sb("acs_all", [P, NT * P], F32)
        acsm_all = sb("acsm_all", [P, NT * P], F32)
        adt_all = sb("adt_all", [P, NT * P], BF16)
        acsT = [sb(f"acsT{i}", [P, NT * P], F32) for i in range(2)]
        hlb = sb("hlb", [P, NT * P], BF16)
        hlf = sb("hlf", [P, NT * P], F32)
        halosb = sb("halosb", [3, CONV_DIM], BF16)
        hsel = sb("hsel", [3, 1], F32)
        hidx = sb("hidx", [3, 1], U32)
        pp1 = ps("pp1", [P, NT * P], F32)
        pp2 = ps("pp2", [P, NT * P], F32)
        pptr = [ps(f"pptr{i}", [P, P], F32) for i in range(2)]
        pht = [ps(f"pht{i}", [P, 4], BF16) for i in range(2)]

        m1f = deps_m + m1_fin
        t_cz = sc.op("pool", lambda g: g.memset(cum[:], 0.0), after=m1f)
        d1 = sc.op("act", lambda s: s.activation(tmpa[:], dtraw[:], AF.Abs, scale=1.0),
                   after=m1f + [dt_done[-1]])
        d2 = sc.op("act", lambda s: s.activation(tmpb[:], tmpa[:], AF.Exp, scale=-1.0), after=[d1])
        d3 = sc.op("act", lambda s: s.activation(tmpb[:], tmpb[:], AF.Ln, bias=1.0, scale=1.0),
                   after=[d2])
        d4 = sc.op("dve", lambda v: v.tensor_scalar(tmpa[:], dtraw[:], 0.0, None, ALU.max),
                   after=[d2] + m1f)
        d5 = sc.op("dve", lambda v: v.tensor_tensor(dtv[:], tmpa[:], tmpb[:], ALU.add),
                   after=[d3, d4])
        d6 = sc.op("act", lambda s: s.activation(lnd[:], dtv[:], AF.Ln, scale=1.0), after=[d5])
        d7 = sc.op("dve", lambda v: v.tensor_tensor(adt_all[:], dtv[:], abc8[:], ALU.mult),
                   after=[d5] + ab8[-1:])
        macs = [sc.op("pe", lambda t_, i=i: t_.matmul(
            pp1[:, i * NW:(i + 1) * NW], ltones[:], adt_all[:, i * NW:(i + 1) * NW],
            start=True, stop=True), after=[d7, t_lt] + m1f) for i in range(2)]
        mrev = [sc.op("pe", lambda t_, i=i: t_.matmul(
            pp2[:, i * NW:(i + 1) * NW], strl[:], adt_all[:, i * NW:(i + 1) * NW],
            start=True, stop=True), after=[d7, t_sl] + macs) for i in range(2)]
        ea = sc.op("dve", lambda v: v.tensor_copy(acs_all[:], pp1[:]), after=macs)
        es = sc.op("act", lambda s: s.activation(sdv[:], pp1[:], AF.Exp, scale=1.0),
                   after=macs + [ea])
        er = sc.op("dve", lambda v: v.tensor_tensor(tmpa[:], pp2[:], lnd[:], ALU.add),
                   after=mrev + [d6, d5])
        ed = sc.op("act", lambda s: s.activation(dsx[:], tmpa[:], AF.Exp, scale=1.0), after=[er])
        em = sc.op("dve", lambda v: v.tensor_tensor(acsm_all[:], acs_all[:], lnd[:],
                                                    ALU.subtract), after=[ea, d6])
        e6 = sc.op("act", lambda s: s.copy(rows[0:1, 0:P] if False else tmpb[0:1, 0:NT * P],
                                           pp2[0:1, :]), after=mrev + [er])
        e7 = sc.op("dve", lambda v: v.tensor_tensor(
            tmpb[0:1, 0:NT * P], acs_all[0:1, :], tmpb[0:1, 0:NT * P], ALU.add),
            after=[e6, ea])
        e8 = sc.op("act", lambda s: s.activation(
            tmpb[0:1, 0:NT * P], tmpb[0:1, 0:NT * P], AF.Exp, scale=1.0), after=[e7])
        e9 = sc.op("act", lambda s: s.copy(hlb[0:1, 0:NT * P], tmpb[0:1, 0:NT * P]), after=[e8])
        ml8 = []
        for i in range(2):
            mm8 = sc.op("pe", lambda t_, i=i: t_.matmul(
                pp1[:, i * NW:(i + 1) * NW], ones_row[0:1, :], hlb[0:1, i * NW:(i + 1) * NW],
                start=True, stop=True), after=[e9, es, t_ones])
            ml8.append(sc.op("dve", lambda v, i=i: v.tensor_copy(
                lam_all[:, i * NW:(i + 1) * NW], pp1[:, i * NW:(i + 1) * NW]),
                after=[mm8, d1, d4, d7]))
        acst_w = []
        evs = []
        for pl in range(2):
            srcb = [acs_all, acsm_all][pl]
            for c8 in range(NT):
                trp = sc.op("pe", lambda t_, pl=pl, c8=c8, srcb=srcb: t_.transpose(
                    pptr[c8 % 2][:], srcb[:, c8 * P:(c8 + 1) * P], idnf[:]),
                    after=[em, ea, t_idnf, ml8[-1]] + ([evs[-2]] if len(evs) >= 2 else []))
                evs.append(sc.op("dve", lambda v, pl=pl, c8=c8: v.tensor_copy(
                    acsT[pl][:, c8 * P:(c8 + 1) * P], pptr[c8 % 2][:]), after=[trp]))
            h1 = sc.op("dve", lambda v, pl=pl: v.tensor_copy(hlb[:], acsT[pl][:]),
                       after=evs[pl * NT:(pl + 1) * NT] + ([acst_w[-1]] if acst_w else [e9]))
            wh = sc.dma("sync", lambda g, pl=pl: g.dma_start(
                out=acst_dram[2 * pl], in_=hlb[:]), after=[h1])
            h2 = sc.op("dve", lambda v, pl=pl: v.tensor_copy(hlf[:], hlb[:]), after=[h1])
            h3 = sc.op("dve", lambda v, pl=pl: v.tensor_tensor(
                hlf[:], acsT[pl][:], hlf[:], ALU.subtract), after=[h2])
            h4 = sc.op("dve", lambda v, pl=pl: v.tensor_copy(hlb[:], hlf[:]), after=[h3, wh])
            wl = sc.dma("sync", lambda g, pl=pl: g.dma_start(
                out=acst_dram[2 * pl + 1], in_=hlb[:]), after=[h4])
            acst_w += [wh, wl]
        t_hi = sc.dma("sync", lambda g: g.dma_start(out=hidx[:], in_=halo_idx_in), after=m1f)
        t_hs = sc.dma("sync", lambda g: g.dma_start(out=hsel[:], in_=halo_sel_in), after=m1f)
        t_hg = sc.dma("pool", lambda g: g.indirect_dma_start(
            halosb[:], None, cc_halo_out[:], bass.IndirectOffsetOnAxis(hidx[:, 0:1], 0)),
            after=[t_cch, t_hi])
        t_hm = sc.op("dve", lambda v: v.tensor_scalar(
            halosb[:], halosb[:], hsel[:, 0:1], None, ALU.mult), after=[t_hg, t_hs])
        t_carry = []
        pc_last = [None, None]
        for m in range(NXB):
            af = [t_hm, t_idn] + m1f + ([pc_last[m % 2]] if pc_last[m % 2] else [])
            trh = sc.op("pe", lambda t_, m=m: t_.transpose(
                pht[m % 2][0:P, 0:3], halosb[0:3, m * P:(m + 1) * P], idn[0:3, 0:3]), after=af)
            tre = sc.op("act", lambda s, m=m: s.copy(
                xbct[:, m * 131:m * 131 + 3], pht[m % 2][0:P, 0:3]), after=[trh])
            pc_last[m % 2] = tre
            t_carry.append(tre)
        pp_fin = sc.frontier()
        pop_to(mPP)

        zx = sb("zx", [P, NSL_XBC * NW], BF16)
        ctmp2 = sb("ctmp2", [P, 20 * P], F32)
        tmpq = sb("tmpq", [P, 20 * P], F32)
        xbcp = sb("xbcp", [P, NXB * P], BF16)
        xtok = sb("xtok", [P, DIN], BF16)
        xs2g = sb("xs2g", [P, 1024], BF16)
        btokc = sb("btokc", [P, P], BF16)
        p_tr = [ps(f"mp_tr{i}", [P, 4 * P], BF16) for i in range(2)]
        p_st = ps("mp_st", [P, 1024], F32)

        st_final = None
        cum_last = {}
        prev_zx = None
        prev_sil_q = [None] * 4
        prev_xtr_q = [None] * 4
        prev_xt_dma = None
        prev_stg = None
        xbcp_v = xbcp[:].rearrange("p (m q) -> p m q", q=P)
        snap_w = []
        for c in range(NT):
            t_snap = sc.dma("pool", lambda g, c=c: g.dma_start(
                out=snap_dram[c], in_=cum[:]),
                after=([t_cz] if c == 0 else [cum_last[g] for g in range(G)]))
            snap_w.append(t_snap)
            t_zx = sc.dma("sync", lambda g, c=c: g.dma_start(
                out=zx[:].rearrange("p (s w) -> p s w", s=NSL_XBC),
                in_=z_dram[NSL_GATE:NSL_IP, c * P:(c + 1) * P, :].rearrange("s r w -> r s w")),
                after=m1f + pp_fin + ([prev_zx] if prev_zx else []))
            carry_q = []
            for q4 in range(4):
                if c > 0:
                    carry_q.append(sc.op("dve", lambda v, q4=q4: v.tensor_copy(
                        xbct_v[:, q4 * 20:(q4 + 1) * 20, 0:3],
                        xbct_v[:, q4 * 20:(q4 + 1) * 20, 128:131]),
                        after=[prev_sil_q[q4]] + ([carry_q[-1]] if carry_q else [])))
                else:
                    carry_q.append(None)
            carry_dep0 = ([t_carry[-1]] + pp_fin) if c == 0 else []
            ptr_last = [None, None]
            tr_evs = []
            for g4 in range(20):
                par = g4 % 2
                q4h = g4 // 5
                trl = None
                for k4 in range(4):
                    m = 4 * g4 + k4
                    af3 = [t_zx, t_idn] + carry_dep0 + \
                          ([carry_q[q4h]] if carry_q[q4h] else []) + \
                          ([ptr_last[par]] if (k4 == 0 and ptr_last[par]) else []) + \
                          ([trl] if trl else [])
                    trl = sc.op("pe", lambda t_, m=m, k4=k4, par=par: t_.transpose(
                        p_tr[par][:, k4 * P:(k4 + 1) * P],
                        zx[:, (m // 4) * NW + (m % 4) * P:(m // 4) * NW + (m % 4 + 1) * P],
                        idn[:]), after=af3)
                af3e = [trl] + carry_dep0 + \
                       ([carry_q[q4h]] if carry_q[q4h] else []) + \
                       ([prev_sil_q[q4h]] if (c > 0 and prev_sil_q[q4h]) else [])
                ev = sc.op("act", lambda s, g4=g4, par=par: s.copy(
                    xbct_v[:, 4 * g4:4 * g4 + 4, 3:131],
                    p_tr[par][:].rearrange("p (m q) -> p m q", q=P)), after=af3e)
                ptr_last[par] = ev
                tr_evs.append(ev)
            prev_zx = tr_evs[-1]
            conv_done = []
            sil_done = []
            prev_ct = None
            for q4 in range(4):
                m0_ = q4 * 20
                cin = xbct_v[:, m0_:m0_ + 20, :]
                c1 = sc.op("dve", lambda v, m0_=m0_, cin=cin: v.tensor_tensor(
                    ctmp2[:].rearrange("p (m q) -> p m q", q=P),
                    cin[:, :, 0:P],
                    convw[:, :].rearrange("p (m t) -> p m t", t=4)[:, m0_:m0_ + 20, 0:1]
                    .to_broadcast([P, 20, 1, P]).rearrange("p m o q -> p (m o) q"),
                    ALU.mult),
                    after=tr_evs[q4 * 5:(q4 + 1) * 5] + ([prev_ct] if prev_ct else []))
                for tp in range(1, 4):
                    c1 = sc.op("dve", lambda v, m0_=m0_, tp=tp, cin=cin: v.tensor_tensor(
                        tmpq[:].rearrange("p (m q) -> p m q", q=P),
                        cin[:, :, tp:tp + P],
                        convw[:, :].rearrange("p (m t) -> p m t", t=4)[:, m0_:m0_ + 20, tp:tp + 1]
                        .to_broadcast([P, 20, 1, P]).rearrange("p m o q -> p (m o) q"),
                        ALU.mult), after=[c1])
                    c1 = sc.op("pool", lambda v: v.tensor_tensor(
                        ctmp2[:], ctmp2[:], tmpq[:], ALU.add), after=[c1])
                c1 = sc.op("dve", lambda v, m0_=m0_: v.tensor_tensor(
                    ctmp2[:].rearrange("p (m q) -> p m q", q=P),
                    ctmp2[:].rearrange("p (m q) -> p m q", q=P),
                    convb[:, m0_:m0_ + 20].to_broadcast([P, 20, P]), ALU.add), after=[c1])
                sil = sc.op("act", lambda s, q4=q4, m0_=m0_: s.activation(
                    xbcp[:, m0_ * P:(m0_ + 20) * P], ctmp2[:], AF.Silu, scale=1.0),
                    after=[c1] + ([sil_done[-1]] if sil_done else []) +
                          ([prev_xtr_q[q4]] if prev_xtr_q[q4] else []))
                prev_ct = sil
                sil_done.append(sil)
            bct_cp = sc.op("act", lambda s, c=c: s.copy(
                BCT[:, c * 2048:(c + 1) * 2048], xbcp[:, 64 * P:80 * P]), after=[sil_done[-1]])
            for q4 in range(4):
                prev_sil_q[q4] = sil_done[q4]
            xe_last = [None, None]
            xevs = []
            for g4 in range(16):
                par = g4 % 2
                trl = None
                for k4 in range(4):
                    m = 4 * g4 + k4
                    af4 = [sil_done[m // 20], t_idn] + \
                          ([xe_last[par]] if (k4 == 0 and xe_last[par]) else []) + \
                          ([trl] if trl else []) + \
                          ([tr_evs[-1]] if (g4 <= 1 and k4 == 0) else [])
                    trl = sc.op("pe", lambda t_, m=m, k4=k4, par=par: t_.transpose(
                        p_tr[par][:, k4 * P:(k4 + 1) * P],
                        xbcp[:, m * P:(m + 1) * P], idn[:]), after=af4)
                ev = sc.op("act", lambda s, g4=g4, par=par: s.copy(
                    xtok[:, g4 * 4 * P:(g4 + 1) * 4 * P], p_tr[par][:]),
                    after=[trl] + ([prev_xt_dma] if (g4 == 0 and prev_xt_dma) else []))
                xe_last[par] = ev
                xevs.append(ev)
            t_xw = sc.dma("sync", lambda g, c=c: g.dma_start(
                out=xt_dram[c], in_=xtok[:]), after=xevs)
            prev_xt_dma = t_xw
            for q4 in range(4):
                prev_xtr_q[q4] = xevs[min(5 * q4 + 4, 15)]
            for g_i in range(G):
                afb = [bct_cp, t_idn] + ([xe_last[g_i % 2]] if xe_last[g_i % 2] else [])
                trb = sc.op("pe", lambda t_, c=c, g_i=g_i: t_.transpose(
                    p_tr[g_i % 2][:, 0:P], BCT[:, c * 2048 + g_i * P:c * 2048 + (g_i + 1) * P],
                    idn[:]), after=afb)
                trbe = sc.op("act", lambda s: s.copy(btokc[:], p_tr[g_i % 2][:, 0:P]),
                             after=[trb] + ([prev_stg] if prev_stg else []))
                xe_last[g_i % 2] = trbe
                x2 = sc.op("dve", lambda v, c=c, g_i=g_i: v.tensor_tensor(
                    xs2g[:].rearrange("p (h e) -> p h e", h=16),
                    xtok[:, g_i * 1024:(g_i + 1) * 1024].rearrange("p (h e) -> p h e", h=16),
                    dsx[:, c * P + g_i * 16:c * P + (g_i + 1) * 16].to_broadcast([P, 16, HD]),
                    ALU.mult),
                    after=[xevs[g_i * 2 + 1], ed] + ([prev_stg] if prev_stg else []))
                mm_last = None
                for hh in range(16):
                    mm_last = sc.op("pe", lambda t_, hh=hh: t_.matmul(
                        p_st[:, hh * HD:(hh + 1) * HD], btokc[:], xs2g[:, hh * HD:(hh + 1) * HD],
                        start=(hh % 8 == 0), stop=(hh % 8 == 7)),
                        after=([trbe, x2] + ([cum_last[g_i]] if (g_i in cum_last) else [])
                               if hh == 0 else []))
                u1 = sc.op("dve", lambda v, c=c, g_i=g_i: v.tensor_tensor(
                    cum[:, g_i * 1024:(g_i + 1) * 1024].rearrange("p (h e) -> p h e", h=16),
                    cum[:, g_i * 1024:(g_i + 1) * 1024].rearrange("p (h e) -> p h e", h=16),
                    lam_all[:, c * P + g_i * 16:c * P + (g_i + 1) * 16].to_broadcast([P, 16, HD]),
                    ALU.mult),
                    after=[ml8[-1], t_cz, snap_w[-1]] + ([cum_last[g_i]] if g_i in cum_last else []))
                u2 = sc.op("dve", lambda v, g_i=g_i: v.tensor_tensor(
                    cum[:, g_i * 1024:(g_i + 1) * 1024],
                    cum[:, g_i * 1024:(g_i + 1) * 1024], p_st[:], ALU.add),
                    after=[u1, mm_last])
                cum_last[g_i] = u2
                prev_stg = u2
            st_final = prev_stg
        fr2a = sc.frontier()
        t_sc = sc.dma("sync", lambda g: g.dma_start(out=cc_st_in[:], in_=cum[:]),
                      after=[cum_last[g] for g in range(G)])
        t_ccs = sc.cc(lambda g: g.collective_compute(
            "AllGather", ALU.bypass, replica_groups=[list(range(NCORES))],
            ins=[cc_st_in[:]], outs=[cc_st_out[:]]), after=[t_sc] + cc_chain)
        cc_chain.append(t_ccs)
        pop_to(mA2)

        mB2 = mark()
        deps_b = deps_m + fr2a + [t_sc]
        xtc = [sb(f"xtc{i}", [P, DIN], BF16) for i in range(2)]
        gatec = sb("gatec", [P, DIN], BF16)
        sinf = sb("sinf", [P, DIN], BF16)
        cumtrue = sb("cumtrue", [P, DIN], BF16)
        ysb = sb("ysb", [P, DIN], F32)
        ytmp = sb("ytmp", [P, NW], F32)
        gml = sb("gml", [P, 2 * 1024], BF16)
        ssb = sb("ssb", [P, 2 * 1024], BF16)
        afla = [sb(f"afla{i}", [1, 2048], BF16) for i in range(4)]
        aflb = [sb(f"aflb{i}", [16, P], BF16) for i in range(4)]
        negid = sb("negid_b", [16, 2048], BF16)
        grst = sb("grst", [P, 16], F32)
        ytst = sb("ytst", [P, 16 * P], BF16)
        ssel = sb("ssel", [P, 1], F32)
        pgm = ps("pgm", [P, P], F32)
        pS = [ps(f"pS{i}", [P, 1024], F32) for i in range(2)]
        pY = ps("pY", [P, 1024], F32)
        p_tr2 = ps("p_tr2", [P, 4 * P], BF16)

        t_ng = sc.dma("pool", lambda g: g.dma_start(out=negid[:], in_=negid_in), after=deps_b)
        t_sg = sc.dma("pool", lambda g: g.indirect_dma_start(
            ysb[:], None, cc_st_out[:], bass.IndirectOffsetOnAxis(sidx[:, 0:1], 0)),
            after=[t_ccs, t_si] + deps_b)
        t_ss = sc.dma("sync", lambda g: g.dma_start(out=ssel[:], in_=st_sel_in), after=deps_b)
        i1 = sc.op("dve", lambda v: v.tensor_scalar(
            sinf[:], ysb[:], ssel[:, 0:1], None, ALU.mult), after=[t_sg, t_ss])

        prev_xt_p = [None, None]
        prev_gate = None
        prev_ytw = None
        prev_ct_use = i1
        prev_sinf = i1
        prev_Yev = i1
        tr2_last = None
        t_af_prev = {}
        flat_prev = {}
        for c in range(NT):
            t_xt = sc.dma("sync", lambda g, c=c: g.dma_start(out=xtc[c % 2][:], in_=xt_dram[c]),
                          after=deps_b + ([prev_xt_p[c % 2]] if prev_xt_p[c % 2] else []))
            t_gt = sc.dma("sync", lambda g, c=c: g.dma_start(
                out=gatec[:].rearrange("p (s w) -> p s w", s=NSL_GATE),
                in_=z_dram[0:NSL_GATE, c * P:(c + 1) * P, :].rearrange("s r w -> r s w")),
                after=deps_b + ([prev_gate] if prev_gate else []))
            if c == 0:
                ctk = sc.op("dve", lambda v: v.tensor_copy(cumtrue[:], sinf[:]),
                            after=[i1, prev_ct_use])
            else:
                sl = sc.op("dve", lambda v, c=c: v.tensor_tensor(
                    sinf[:].rearrange("p (h e) -> p h e", h=NH),
                    sinf[:].rearrange("p (h e) -> p h e", h=NH),
                    lam_all[:, (c - 1) * P:c * P].to_broadcast([P, NH, HD]),
                    ALU.mult), after=[prev_sinf])
                prev_sinf = sl
                ld = sc.dma("sync", lambda g, c=c: g.dma_start(
                    out=cumtrue[:], in_=snap_dram[c]),
                    after=[snap_w[c], prev_ct_use] + deps_b)
                ctk = sc.op("dve", lambda v: v.tensor_tensor(
                    cumtrue[:], cumtrue[:], sinf[:], ALU.add), after=[ld, sl])
            t_af = {}

            def load_flats(g_i, extra):
                par_g = g_i % 2
                for hl in range(2):
                    slota = par_g * 2 + hl
                    paf = [x for x in (t_af_prev.get(("f", slota)),
                                       t_af_prev.get(("f2", slota))) if x] + extra
                    t_af[(g_i, "a", hl)] = sc.dma("pool", lambda g, c=c, g_i=g_i, hl=hl, slota=slota:
                        g.dma_start(
                            out=afla[slota][:].rearrange("o (h j) -> o h j", h=16),
                            in_=acst_dram[hl][g_i * 16:(g_i + 1) * 16, c * P:(c + 1) * P]
                            .rearrange("(o h) j -> o h j", o=1)),
                        after=deps_b + paf)
                    t_af[(g_i, "b", hl)] = sc.dma("sync", lambda g, c=c, g_i=g_i, hl=hl, slota=slota:
                        g.dma_start(
                            out=aflb[slota][:],
                            in_=acst_dram[2 + hl][g_i * 16:(g_i + 1) * 16, c * P:(c + 1) * P]),
                        after=deps_b + paf)

            load_flats(0, list(flat_prev.get(0, [])))
            load_flats(1, list(flat_prev.get(1, [])))
            prev_S = [None, None]
            gml_use = [None, None]
            y_evs = []
            for g_i in range(G):
                gm = sc.op("pe", lambda t_, c=c, g_i=g_i: t_.matmul(
                    pgm[:], BCT[:, c * 2048 + g_i * P:c * 2048 + (g_i + 1) * P],
                    BCT[:, c * 2048 + 1024 + g_i * P:c * 2048 + 1024 + (g_i + 1) * P],
                    start=True, stop=True),
                    after=deps_b + [x for x in (gml_use[0], gml_use[1]) if x])
                for b2 in range(2):
                    h0 = g_i * 16 + b2 * 8
                    par = (2 * g_i + b2) % 2
                    pg2 = (g_i % 2) * 2
                    afS = [t_af[(g_i, "a", 0)], t_af[(g_i, "a", 1)],
                           t_af[(g_i, "b", 0)], t_af[(g_i, "b", 1)], t_ones, t_ng, t_m4c] + \
                          [x for x in (prev_S[par], gml_use[par]) if x]
                    r1 = None
                    for nn in range(2):
                        cb = (b2 * 8 + nn * 4) * P
                        r1 = sc.op("pe", lambda t_, pg2=pg2, par=par, cb=cb, nn=nn: t_.matmul(
                            pS[par][:, nn * NW:(nn + 1) * NW], ones_row[0:1, :],
                            afla[pg2][0:1, cb:cb + 4 * P],
                            start=True, stop=False), after=(afS if nn == 0 else [r1]))
                        r1 = sc.op("pe", lambda t_, pg2=pg2, par=par, cb=cb, nn=nn: t_.matmul(
                            pS[par][:, nn * NW:(nn + 1) * NW], ones_row[0:1, :],
                            afla[pg2 + 1][0:1, cb:cb + 4 * P],
                            start=False, stop=False), after=[r1])
                        r1 = sc.op("pe", lambda t_, pg2=pg2, par=par, cb=cb, nn=nn: t_.matmul(
                            pS[par][:, nn * NW:(nn + 1) * NW], aflb[pg2][:, :],
                            negid[:, cb:cb + 4 * P],
                            start=False, stop=False), after=[r1])
                        r1 = sc.op("pe", lambda t_, pg2=pg2, par=par, cb=cb, nn=nn: t_.matmul(
                            pS[par][:, nn * NW:(nn + 1) * NW], aflb[pg2 + 1][:, :],
                            negid[:, cb:cb + 4 * P],
                            start=False, stop=False), after=[r1])
                        r1 = sc.op("pe", lambda t_, par=par, nn=nn: t_.matmul(
                            pS[par][:, nn * NW:(nn + 1) * NW], idn[:], mask4[:, 0:NW],
                            start=False, stop=True), after=[r1, t_idn])
                    sE = sc.op("act", lambda s, par=par: s.activation(
                        ssb[:, par * 1024:(par + 1) * 1024], pS[par][:],
                        AF.Exp, scale=1.0), after=[r1] +
                        ([gml_use[par]] if gml_use[par] else []))
                    prev_S[par] = sE
                    gu = []
                    for hh in range(8):
                        gu.append(sc.op("dve", lambda v, hh=hh, par=par: v.tensor_tensor(
                            gml[:, par * 1024 + hh * P:par * 1024 + (hh + 1) * P],
                            ssb[:, par * 1024 + hh * P:par * 1024 + (hh + 1) * P],
                            pgm[:], ALU.mult),
                            after=[sE, gm] + ([gml_use[par]] if (hh == 0 and gml_use[par]) else [])))
                    ym = None
                    for hh in range(8):
                        ym = sc.op("pe", lambda t_, c=c, hh=hh, par=par, h0=h0: t_.matmul(
                            pY[:, hh * HD:(hh + 1) * HD],
                            gml[:, par * 1024 + hh * P:par * 1024 + (hh + 1) * P],
                            xtc[c % 2][:, (h0 + hh) * HD:(h0 + hh + 1) * HD],
                            start=(hh == 0), stop=(hh == 7)),
                            after=[gu[hh], t_xt] + ([prev_Yev] if (hh == 0 and prev_Yev) else []))
                    for hh in range(8):
                        ym = sc.op("pe", lambda t_, c=c, g_i=g_i, hh=hh, b2=b2, h0=h0: t_.matmul(
                            pY[:, NW + hh * HD:NW + (hh + 1) * HD],
                            BCT[:, c * 2048 + 1024 + g_i * P:c * 2048 + 1024 + (g_i + 1) * P],
                            cumtrue[:, (h0 + hh) * HD:(h0 + hh + 1) * HD],
                            start=(hh == 0), stop=(hh == 7)), after=[ym, ctk])
                    gml_use[par] = ym
                    v1 = sc.op("dve", lambda v, c=c, h0=h0: v.tensor_tensor(
                        ytmp[:].rearrange("p (h e) -> p h e", h=8),
                        pY[:, NW:2 * NW].rearrange("p (h e) -> p h e", h=8),
                        sdv[:, c * P + h0:c * P + h0 + 8].to_broadcast([P, 8, HD]),
                        ALU.mult), after=[ym])
                    v2 = sc.op("dve", lambda v, h0=h0: v.tensor_tensor(
                        ysb[:, h0 * HD:(h0 + 8) * HD], pY[:, 0:NW], ytmp[:], ALU.add),
                        after=[v1])
                    v3 = sc.op("dve", lambda v, c=c, h0=h0: v.tensor_tensor(
                        ytmp[:].rearrange("p (h e) -> p h e", h=8),
                        xtc[c % 2][:, h0 * HD:(h0 + 8) * HD].rearrange("p (h e) -> p h e", h=8),
                        dbc[:, h0:h0 + 8].to_broadcast([P, 8, HD]), ALU.mult),
                        after=[v2, t_xt])
                    v4 = sc.op("dve", lambda v, h0=h0: v.tensor_tensor(
                        ysb[:, h0 * HD:(h0 + 8) * HD], ysb[:, h0 * HD:(h0 + 8) * HD],
                        ytmp[:], ALU.add), after=[v3])
                    prev_Yev = v4
                    y_evs.append(v4)
                for _hl in range(2):
                    t_af_prev[("f", (g_i % 2) * 2 + _hl)] = prev_S[0]
                    t_af_prev[("f2", (g_i % 2) * 2 + _hl)] = prev_S[1]
                nxt = [x for x in prev_S if x]
                if g_i + 2 < G:
                    load_flats(g_i + 2, nxt)
                else:
                    flat_prev[g_i - 6] = nxt
            prev_ct_use = y_evs[-1]
            sg = sc.op("act", lambda s: s.activation(gatec[:], gatec[:], AF.Silu, scale=1.0),
                       after=[t_gt])
            gm2 = sc.op("dve", lambda v: v.tensor_tensor(ysb[:], ysb[:], gatec[:], ALU.mult),
                        after=[sg] + y_evs[-2:])
            nsq = []
            for g_i in range(G):
                nsq.append(sc.op("act", lambda s, g_i=g_i: s.activation(
                    gatec[:, g_i * 1024:(g_i + 1) * 1024], ysb[:, g_i * 1024:(g_i + 1) * 1024],
                    AF.Square, accum_out=grst[:, g_i:g_i + 1]), after=[gm2, sg]))
            n1 = sc.op("act", lambda s: s.activation(
                grst[:, 8:16], grst[:, 0:8], AF.Ln, bias=EPS, scale=1.0 / 1024), after=nsq)
            n2 = sc.op("act", lambda s: s.activation(
                grst[:, 8:16], grst[:, 8:16], AF.Exp, scale=-0.5), after=[n1])
            napp = []
            for g_i in range(G):
                napp.append(sc.op("dve", lambda v, g_i=g_i: v.tensor_scalar(
                    gatec[:, g_i * 1024:(g_i + 1) * 1024], ysb[:, g_i * 1024:(g_i + 1) * 1024],
                    grst[:, 8 + g_i:9 + g_i], None, ALU.mult),
                    after=[n2, nsq[g_i]]))
            prev_gate = napp[-1]
            yb = napp[-1]
            prev_xt_p[c % 2] = y_evs[-1]
            for half in range(4):
                yws = []
                for q4 in range(4):
                    trl = None
                    for k4 in range(4):
                        m = half * 16 + q4 * 4 + k4
                        trl = sc.op("pe", lambda t_, m=m, k4=k4: t_.transpose(
                            p_tr2[:, k4 * P:(k4 + 1) * P], gatec[:, m * P:(m + 1) * P], idn[:]),
                            after=[napp[m // 8], t_idn] + ([tr2_last] if (k4 == 0 and tr2_last) else []) +
                                  ([trl] if trl else []))
                    m0q = half * 16 + q4 * 4
                    ye = sc.op("dve", lambda v, q4=q4, m0q=m0q: v.tensor_tensor(
                        ytst[:, q4 * 4 * P:(q4 + 1) * 4 * P].rearrange("p (m q) -> p m q", q=P),
                        p_tr2[:].rearrange("p (m q) -> p m q", q=P),
                        nwc[:, m0q:m0q + 4].to_broadcast([P, 4, P]), ALU.mult),
                        after=[trl, t_nw] + ([prev_ytw] if (q4 == 0 and prev_ytw) else []))
                    yws.append(ye)
                    tr2_last = ye
                prev_ytw = sc.dma("sync", lambda g, c=c, half=half: g.dma_start(
                    out=yt_dram[c][half * 16:(half + 1) * 16].rearrange("m p q -> p m q"),
                    in_=ytst[:].rearrange("p (m q) -> p m q", m=16)), after=yws)
        fr2b = sc.frontier()
        pop_to(mS)

        mC2 = mark()
        deps_c = deps + fr2b
        ytf = sb("ytf", [P, 64 * TOK], BF16)
        wob = [sb(f"wobm{i}", [P, 64 * P], BF16) for i in range(2)]
        obf = sb("obf", [P, TOK], BF16)
        stf = [sb(f"stf{i}", [P, NW], F32) for i in range(4)]
        pb_ = [ps(f"mpb{i}", [P, NW], F32) for i in range(4)]
        p_tr3 = [ps(f"p_tr3{i}", [P, NW], BF16) for i in range(2)]
        if dst_rm:
            t_init = sc.dma("sync", lambda g: g.dma_start(
                out=dst_dram,
                in_=(src_dram if src_rm else
                     src_dram.rearrange("t s p w -> (t p) (s w)"))), after=deps)
        else:
            t_init = None
            for tt_ in range(NT):
                t_init = sc.dma("sync", lambda g, tt_=tt_: g.dma_start(
                    out=dst_dram[tt_],
                    in_=(src_dram[tt_ * P:(tt_ + 1) * P, :].rearrange(
                        "p (s w) -> s p w", w=NW) if src_rm else src_dram[tt_])),
                    after=deps)
        t_ytf_l = [sc.dma(("sync", "pool", "act")[c % 3], lambda g, c=c: g.dma_start(
            out=ytf[:].rearrange("p (m cq) -> p m cq", m=64)[:, :, c * P:(c + 1) * P],
            in_=yt_dram[c].rearrange("m p q -> p m q")), after=deps_c) for c in range(NT)]
        t_ytf = t_ytf_l[-1]
        buf_use3 = [None, None]
        prev_stf = [None, None]
        stf_use = [None, None, None, None]
        prev_obf = None
        for m in range(32):
            af = list(deps_c)
            if buf_use3[m % 2] is not None:
                af.append(buf_use3[m % 2])
            t_w = sc.dma("pool", lambda g, m=m: g.dma_start(out=wob[m % 2][:], in_=w_ob[m]),
                         after=af)
            mml = None
            for kt in range(64):
                for ns2 in range(2):
                    af2 = []
                    if kt == 0 and ns2 == 0:
                        af2 = [t_w] + t_ytf_l + ([prev_obf] if prev_obf else [])
                    mml = sc.op("pe", lambda t_, m=m, kt=kt, ns2=ns2: t_.matmul(
                        pb_[2 * (m % 2) + ns2][:], wob[m % 2][:, kt * P:(kt + 1) * P],
                        ytf[:, kt * TOK + ns2 * NW:kt * TOK + (ns2 + 1) * NW],
                        start=(kt == 0), stop=(kt == 63)), after=af2)
            buf_use3[m % 2] = mml
            ev1 = sc.op("act", lambda s, m=m: s.copy(obf[:, 0:NW], pb_[2 * (m % 2)][:]),
                        after=[mml] + ([prev_obf] if prev_obf else []))
            ev2 = sc.op("act", lambda s, m=m: s.copy(obf[:, NW:TOK], pb_[2 * (m % 2) + 1][:]),
                        after=[mml, ev1])
            obf_reads = []
            for half in range(2):
                sidx4 = (2 * m + half) % 4
                trl = None
                for q4 in range(4):
                    t_t = half * 4 + q4
                    trl = sc.op("pe", lambda t_, t_t=t_t, q4=q4, half=half: t_.transpose(
                        p_tr3[half][:, q4 * P:(q4 + 1) * P], obf[:, t_t * P:(t_t + 1) * P],
                        idn[:]),
                        after=[ev1, ev2, t_idn] +
                              ([prev_stf[half]] if (q4 == 0 and prev_stf[half]) else []) +
                              ([trl] if trl else []))
                sv = sc.op("dve", lambda v, half=half, sidx4=sidx4: v.tensor_copy(
                    stf[sidx4][:], p_tr3[half][:]),
                    after=[trl] + ([stf_use[sidx4]] if stf_use[sidx4] else []))
                obf_reads.append(trl)
                if dst_rm:
                    wdma = sc.dma("pool", lambda g, m=m, half=half, sidx4=sidx4: g.dma_start(
                        out=dst_dram.rearrange("(t p) f -> p t f", p=P)[
                            :, half * 4:(half + 1) * 4, m * P:(m + 1) * P],
                        in_=stf[sidx4][:].rearrange("p (t q) -> p t q", t=4),
                        accum_op=ALU.add), after=[sv, t_init])
                else:
                    wdma = sc.dma("pool", lambda g, m=m, half=half, sidx4=sidx4: g.dma_start(
                        out=dst_dram[half * 4:(half + 1) * 4, m // 4, :,
                                     (m % 4) * P:(m % 4 + 1) * P].rearrange("t p q -> p t q"),
                        in_=stf[sidx4][:].rearrange("p (t q) -> p t q", t=4),
                        accum_op=ALU.add), after=[sv, t_init])
                prev_stf[half] = sv
                stf_use[sidx4] = wdma
            prev_obf = obf_reads[-1]
        fr = sc.frontier()
        pop_to(m0)
        return fr

    def emit_attn(src_dram, src_rm, dst_dram, dst_rm, deps):
        m0 = mark()
        qT = sb("qT", [P, 8 * TOK], BF16)
        aot = sb("aot", [P, 8 * TOK], BF16)
        bo_row = sb("bo_row", [1, D], BF16)
        bqkv_row = sb("bqkv_row", [1, 3072], BF16)
        rcp = sb("rcp", [P, NT], F32)
        mA = mark()
        xnT = sb("xnT_a", [P, KT * TOK], BF16)
        mL = mark()
        ln_bufs = mk_ln_bufs("aln")
        xn_done, t_norm = emit_ln(src_dram, src_rm, ln_bufs, xnT, deps)
        pop_to(mL)
        deps = list(deps) + [xn_done[-1], t_norm[-1]]

        t_bo = sc.dma("pool", lambda g: g.dma_start(out=bo_row[:], in_=bo_in), after=deps)
        t_bq = sc.dma("pool", lambda g: g.dma_start(out=bqkv_row[:], in_=bqkv_in), after=deps)

        ktb = sb("ktb", [P, 8 * TOK], BF16)
        vst = sb("vst", [P, 8 * TOK], BF16)
        qtmp = sb("qtmp", [P, NW], BF16)
        wqb = [sb(f"wq{i}", [P, 16 * NW], BF16) for i in range(3)]
        pqk = [ps(f"pqk{i}", [P, NW], F32) for i in range(2)]
        ptq = [ps(f"ptq{i}", [P, P], BF16) for i in range(2)]
        buf_use = [None, None, None]
        prev_tr = [None, None]
        prev_tmp = None
        prev_ev = {}
        tr_done = []
        for s_i in range(6):
            t_wh = [None, None]
            for hf in range(2):
                bidx = (2 * s_i + hf) % 3
                af = list(deps)
                if buf_use[bidx] is not None:
                    af.append(buf_use[bidx])
                t_wh[hf] = sc.dma("pool", lambda g, s_i=s_i, hf=hf, bidx=bidx: g.dma_start(
                    out=wqb[bidx][:], in_=w_qkv[s_i][:, hf * 16 * NW:(hf + 1) * 16 * NW]),
                    after=af)
            for t in range(NT):
                gi = s_i * NT + t
                pb = pqk[gi % 2]
                af2 = [t_wh[0]] + deps
                if gi >= 2:
                    af2.append(prev_ev[gi % 2])
                mmlast = None
                for kt in range(KT):
                    hf = kt // 16
                    bidx = (2 * s_i + hf) % 3
                    mmlast = sc.op("pe", lambda t_, t=t, kt=kt, pb=pb, bidx=bidx: t_.matmul(
                        pb[:], xnT[:, kt * TOK + t * P:kt * TOK + (t + 1) * P],
                        wqb[bidx][:, (kt % 16) * NW:(kt % 16 + 1) * NW],
                        start=(kt == 0), stop=False),
                        after=(af2 if kt == 0 else ([t_wh[1]] if kt == 16 else [])))
                    if t == NT - 1 and kt % 16 == 15:
                        buf_use[bidx] = mmlast
                mb = sc.op("pe", lambda t_, s_i=s_i, pb=pb: t_.matmul(
                    pb[:], ones_row[0:1, :], bqkv_row[0:1, s_i * NW:(s_i + 1) * NW],
                    start=False, stop=True), after=[mmlast, t_bq, t_ones])
                if s_i < 4:
                    scl = 0.03125 if s_i < 2 else 1.0
                    ev = sc.op("dve", lambda v, pb=pb, scl=scl: v.tensor_scalar(
                        qtmp[:], pb[:], scl, None, ALU.mult),
                        after=[mb] + ([prev_tmp] if prev_tmp else []))
                    trl = []
                    for q4 in range(4):
                        dt = (s_i % 2) * 4 + q4
                        af3 = [ev, t_idn] + ([prev_tr[q4 % 2]] if prev_tr[q4 % 2] else [])
                        tr = sc.op("pe", lambda t_, q4=q4: t_.transpose(
                            ptq[q4 % 2][:], qtmp[:, q4 * P:(q4 + 1) * P], idn[:]), after=af3)
                        dst_sb = qT if s_i < 2 else ktb
                        tev = sc.op("act", lambda s, dt=dt, t=t, q4=q4, dst_sb=dst_sb: s.copy(
                            dst_sb[:, dt * TOK + t * P:dt * TOK + (t + 1) * P],
                            ptq[q4 % 2][:]), after=[tr])
                        prev_tr[q4 % 2] = tev
                        trl.append(tev)
                    prev_tmp = trl[-1]
                    prev_ev[gi % 2] = trl[-1]
                    tr_done.extend(trl)
                else:
                    ev = sc.op("act", lambda s, s_i=s_i, t=t, pb=pb: s.copy(
                        vst[:, t * TOK + (s_i - 4) * NW: t * TOK + (s_i - 3) * NW], pb[:]),
                        after=[mb])
                    prev_ev[gi % 2] = ev
                    tr_done.append(ev)

        t_c1 = sc.dma("sync", lambda g: g.dma_start(
            out=cc_kv_in[0:1024].rearrange("(dt p) k -> p dt k", p=P),
            in_=ktb[:].rearrange("p (dt k) -> p dt k", dt=8)), after=tr_done)
        t_c2 = sc.dma("sync", lambda g: g.dma_start(
            out=cc_kv_in[1024:2048].rearrange("(t p) d -> p t d", p=P),
            in_=vst[:].rearrange("p (t d) -> p t d", t=8)), after=tr_done)
        af_cc = [t_c1, t_c2] + cc_chain
        t_cc = sc.cc(lambda g: g.collective_compute(
            "AllGather", ALU.bypass, replica_groups=[list(range(NCORES))],
            ins=[cc_kv_in[:]], outs=[cc_kv_out[:]]), after=af_cc)
        cc_chain.append(t_cc)
        pop_to(mA)

        mB = mark()
        ktf = sb("ktf", [P, 8 * 2048], BF16)
        vf = sb("vf", [P, 16 * TOK], BF16)
        kvidx = sb("kvidx", [P, 32], U32)
        pexp = [sb(f"pexp{i}", [P, 2048], BF16) for i in range(2)]
        ptb = [sb(f"ptb{i}", [P, 16 * P], BF16) for i in range(2)]
        avb = [sb(f"avb{i}", [P, TOK], BF16) for i in range(2)]
        smax = sb("smax", [P, 2 * NT], F32)
        psc = ps("psc", [P, 2048], F32)
        pav = ps("pav", [P, TOK], F32)
        ptr3 = [ps(f"ptr3{i}", [P, P], BF16) for i in range(2)]
        t_ki = sc.dma("sync", lambda g: g.dma_start(out=kvidx[:], in_=kv_idx_in),
                      after=deps + [t_c1, t_c2])
        t_g = []
        for j in range(16):
            dt, hf = j % 8, j // 8
            t_g.append(sc.dma("pool", lambda g, dt=dt, hf=hf, j=j: g.indirect_dma_start(
                ktf[:, dt * 2048 + hf * 1024: dt * 2048 + (hf + 1) * 1024], None,
                cc_kv_out[:], bass.IndirectOffsetOnAxis(kvidx[:, j:j + 1], 0)),
                after=[t_cc, t_ki]))
        for j in range(16):
            t_g.append(sc.dma("pool", lambda g, j=j: g.indirect_dma_start(
                vf[:, j * TOK:(j + 1) * TOK], None,
                cc_kv_out[:], bass.IndirectOffsetOnAxis(kvidx[:, 16 + j:17 + j], 0)),
                after=[t_cc, t_ki]))
        prev_exp = None
        prev_pt = [None, None]
        av_read = [None, None]
        prev_x4 = None
        t_aot = []
        prev_aot = [None, None]
        for t in range(NT):
            af = t_g + ([prev_exp] if prev_exp else [])
            mml = None
            for ns in range(4):
                for dt in range(8):
                    mml = sc.op("pe", lambda t_, t=t, ns=ns, dt=dt: t_.matmul(
                        psc[:, ns * NW:(ns + 1) * NW],
                        qT[:, dt * TOK + t * P:dt * TOK + (t + 1) * P],
                        ktf[:, dt * 2048 + ns * NW: dt * 2048 + (ns + 1) * NW],
                        start=(dt == 0), stop=(dt == 7)),
                        after=(af if (ns == 0 and dt == 0) else []))
            sm = smax[:, 2 * t:2 * t + 2]
            x1 = sc.op("dve", lambda v, sm=sm: v.reduce_max(
                sm[:, 0:1], psc[:], AX, negate=True), after=[mml])
            x2 = sc.op("act", lambda s, t=t, sm=sm: s.activation(
                pexp[t % 2][:], psc[:], AF.Exp, bias=sm[:, 0:1], scale=1.0,
                accum_out=sm[:, 1:2]),
                after=[x1] + ([prev_pt[t % 2]] if prev_pt[t % 2] else []))
            prev_exp = x2
            x3 = sc.op("dve", lambda v, t=t, sm=sm: v.reciprocal(
                rcp[:, t:t + 1], sm[:, 1:2]), after=[x2])
            ptl = []
            for k2 in range(16):
                af4 = [x2, t_idn] + ([ptl[-2]] if len(ptl) >= 2 else [])
                tr = sc.op("pe", lambda t_, t=t, k2=k2: t_.transpose(
                    ptr3[k2 % 2][:], pexp[t % 2][:, k2 * P:(k2 + 1) * P], idn[:]), after=af4)
                tev = sc.op("act", lambda s, t=t, k2=k2: s.copy(
                    ptb[t % 2][:, k2 * P:(k2 + 1) * P], ptr3[k2 % 2][:]),
                    after=[tr] + ([prev_x4] if (k2 == 0 and prev_x4) else []))
                ptl.append(tev)
            prev_pt[t % 2] = ptl[-1]
            afv = ptl + ([prev_x4] if prev_x4 else [])
            mv = None
            for k2 in range(16):
                for ns in range(2):
                    mv = sc.op("pe", lambda t_, t=t, k2=k2, ns=ns: t_.matmul(
                        pav[:, ns * NW:(ns + 1) * NW],
                        ptb[t % 2][:, k2 * P:(k2 + 1) * P],
                        vf[:, k2 * TOK + ns * NW: k2 * TOK + (ns + 1) * NW],
                        start=(k2 == 0), stop=(k2 == 15)),
                        after=(afv if (k2 == 0 and ns == 0) else []))
            x4 = sc.op("dve", lambda v, t=t: v.tensor_scalar(
                avb[t % 2][:], pav[:], rcp[:, t:t + 1], None, ALU.mult),
                after=[mv, x3] + ([av_read[t % 2]] if av_read[t % 2] else []))
            prev_x4 = x4
            for dt in range(8):
                af5 = [x4, t_idn] + ([prev_aot[dt % 2]] if prev_aot[dt % 2] else [])
                tr = sc.op("pe", lambda t_, t=t, dt=dt: t_.transpose(
                    ptr3[dt % 2][:], avb[t % 2][:, dt * P:(dt + 1) * P], idn[:]), after=af5)
                tev = sc.op("act", lambda s, t=t, dt=dt: s.copy(
                    aot[:, dt * TOK + t * P:dt * TOK + (t + 1) * P], ptr3[dt % 2][:]), after=[tr])
                prev_aot[dt % 2] = tev
                t_aot.append(tev)
            av_read[t % 2] = t_aot[-1]
        pop_to(mB)

        mC = mark()
        deps_c = list(deps) + sc.frontier()
        wob = [sb(f"wo2_{i}", [P, 8 * NW], BF16) for i in range(2)]
        stage = [sb(f"as{i}", [P, NW], F32) for i in range(4)]
        resb = [sb(f"ar{i}", [P, NW], F32) for i in range(4)]
        pmm = [ps(f"apm{i}", [P, NW], F32) for i in range(4)]
        buf_use2 = [None, None]
        prev_stage_use = {}
        for s_i in range(8):
            af = list(deps_c)
            if buf_use2[s_i % 2] is not None:
                af.append(buf_use2[s_i % 2])
            t_w = sc.dma("pool", lambda g, s_i=s_i: g.dma_start(
                out=wob[s_i % 2][:], in_=w_o2[s_i]), after=af)
            mml = None
            for t in range(NT):
                gi = s_i * NT + t
                pb = pmm[gi % 4]
                af2 = [t_w] + t_aot[-1:]
                if gi >= 4:
                    af2.append(prev_stage_use[gi % 4])
                for kt in range(8):
                    mml = sc.op("pe", lambda t_, t=t, kt=kt, pb=pb, s_i=s_i: t_.matmul(
                        pb[:], aot[:, kt * TOK + t * P:kt * TOK + (t + 1) * P],
                        wob[s_i % 2][:, kt * NW:(kt + 1) * NW],
                        start=(kt == 0), stop=False), after=(af2 if kt == 0 else []))
                mb = sc.op("pe", lambda t_, s_i=s_i, pb=pb: t_.matmul(
                    pb[:], ones_row[0:1, :], bo_row[0:1, s_i * NW:(s_i + 1) * NW],
                    start=False, stop=True), after=[mml, t_bo])
                rl = sc.dma("sync", lambda g, s_i=s_i, t=t, gi=gi: g.dma_start(
                    out=resb[gi % 4][:],
                    in_=(src_dram[t * P:(t + 1) * P, s_i * NW:(s_i + 1) * NW] if src_rm
                         else src_dram[t, s_i])),
                    after=(deps_c if gi < 4 else [prev_stage_use[gi % 4]]))
                ev = sc.op("dve", lambda v, gi=gi, pb=pb: v.tensor_tensor(
                    stage[gi % 4][:], pb[:], resb[gi % 4][:], ALU.add), after=[mb, rl])
                st = sc.dma("sync", lambda g, s_i=s_i, t=t, gi=gi: g.dma_start(
                    out=(dst_dram[t * P:(t + 1) * P, s_i * NW:(s_i + 1) * NW] if dst_rm
                         else dst_dram[t, s_i]), in_=stage[gi % 4][:]), after=[ev])
                prev_stage_use[gi % 4] = st
            buf_use2[s_i % 2] = mml
        fr = sc.frontier()
        pop_to(m0)
        return fr

    cc_chain = []
    deps = [t_idn, t_idnf, t_ones, t_mones, t_c0, t_c1]
    cur_src = x_ext
    cur_rm = True
    for i, name in enumerate(seq):
        last = (i == len(seq) - 1)
        dst = out_ext if last else streams_d[i]
        dst_rm = last
        if name in ("ffn", "ffn2"):
            deps = emit_ffn(cur_src, cur_rm, dst, dst_rm, deps)
        elif name == "attn":
            deps = emit_attn(cur_src, cur_rm, dst, dst_rm, deps)
        elif name == "mamba":
            deps = emit_mamba(cur_src, cur_rm, dst, dst_rm, deps)
        cur_src, cur_rm = dst, dst_rm

    blk = nc.Block()
    block = blk.__enter__()
    sc.finalize(nc, block, ctx_stack)
    blk.__exit__(None, None, None)
    while ctx_stack:
        ctx_stack.pop()[1].__exit__(None, None, None)
    return nc



def _prep(inputs, seq):
    ip = np.ascontiguousarray(inputs["L0_attn_a_in_proj_w"].T)
    arrs = {}
    arrs["w_ip"] = _pack_a(ip[:, :NSL_IP * NW], NSL_IP)
    arrs["w_dt"] = _pack_a(ip[:, NSL_IP * NW:], 1, nw=P)[0]
    wo = np.ascontiguousarray(inputs["L0_attn_a_out_proj_w"].T)
    arrs["w_ob"] = _pack_b(wo, 32)
    arrs["w_base"] = _pack_a(np.ascontiguousarray(inputs["L0_ffn_base_W"].T), 8)
    AsT = np.ascontiguousarray(inputs["L0_ffn_A"].reshape(P, D).T)
    WrT = np.ascontiguousarray(inputs["L0_ffn_router_W"].T)
    arrs["w_ra"] = _pack_a(np.concatenate([AsT, WrT], axis=1), 1, nw=143)[0]
    arrs["w_lb"] = np.ascontiguousarray(inputs["L0_ffn_B"].transpose(0, 2, 1).reshape(P, D))
    arrs["w_qkv"] = _pack_a(np.ascontiguousarray(inputs["L0_attn_b_Wqkv"]), 6)
    arrs["w_o2"] = _pack_a(np.ascontiguousarray(inputs["L0_attn_b_Wo"]), 8)
    arrs["convw"] = np.ascontiguousarray(
        inputs["L0_attn_a_conv_w"].reshape(CONV_DIM, 4).reshape(NXB, P, 4)
        .transpose(1, 0, 2).reshape(P, NXB * 4))
    arrs["convb"] = np.ascontiguousarray(
        inputs["L0_attn_a_conv_b"].reshape(NXB, P).T)
    arrs["alog"] = inputs["L0_attn_a_A_log"].reshape(1, NH)
    arrs["dvec"] = inputs["L0_attn_a_D"].reshape(1, NH)
    arrs["dtb"] = inputs["L0_attn_a_dt_bias"].reshape(1, NH)
    arrs["normw"] = np.ascontiguousarray(inputs["L0_attn_a_norm_w"].reshape(64, P).T)
    arrs["baseb"] = inputs["L0_ffn_base_b"].reshape(1, D)
    arrs["rb"] = inputs["L0_ffn_router_b"].reshape(1, 15)
    arrs["bqkv"] = inputs["L0_attn_b_bqkv"].reshape(1, 3072)
    arrs["bo"] = inputs["L0_attn_b_bo"].reshape(1, D)
    arrs["emap"] = np.ascontiguousarray(inputs["L0_ffn_expert_map"].reshape(15, 1))
    arrs["ident"] = np.eye(P, dtype=np.float32)
    jj, ii = np.meshgrid(np.arange(P), np.arange(P), indexing="ij")
    arrs["ltones"] = (jj <= ii).astype(np.float32)
    arrs["strl"] = (jj > ii).astype(np.float32)
    arrs["maskneg"] = np.where(jj <= ii, 0.0, NEG_BIG).astype(np.float32)
    e2 = np.zeros((8, P), dtype=np.float32)
    for e in range(8):
        e2[e, e * 16:(e + 1) * 16] = 2.0
    arrs["expand2"] = e2
    arrs["iota8"] = np.tile(np.arange(8, dtype=np.float32), (15, 1))
    ng = np.zeros((16, 2048), dtype=np.float32)
    for k in range(16):
        ng[k, k * 128:(k + 1) * 128] = -1.0
    arrs["negid16"] = ng

    x = np.ascontiguousarray(inputs["L0_attn_a_hidden_states"].astype(np.float32))
    in_maps = []
    for c in range(NCORES):
        b, h = c // 2, c % 2
        m = {k: v for k, v in arrs.items()}
        m["x"] = np.ascontiguousarray(x[b, h * TOK:(h + 1) * TOK, :])
        pair = c ^ 1
        m["halo_idx"] = (np.arange(3, dtype=np.uint32) + (c - 1) * 3 if h == 1
                         else np.zeros(3, dtype=np.uint32)).reshape(3, 1).astype(np.uint32)
        m["halo_sel"] = np.full((3, 1), 1.0 if h == 1 else 0.0, dtype=np.float32)
        m["st_idx"] = ((np.arange(P, dtype=np.uint32) + (c - 1) * P) if h == 1
                       else np.arange(P, dtype=np.uint32)).reshape(P, 1).astype(np.uint32)
        m["st_sel"] = np.full((P, 1), 1.0 if h == 1 else 0.0, dtype=np.float32)
        kvi = np.zeros((P, 32), dtype=np.uint32)
        e_core, o_core = 2 * b, 2 * b + 1
        for t in range(8):
            kvi[:, t] = e_core * 2048 + t * P + np.arange(P)
            kvi[:, 8 + t] = o_core * 2048 + t * P + np.arange(P)
            kvi[:, 16 + t] = e_core * 2048 + 1024 + t * P + np.arange(P)
            kvi[:, 24 + t] = o_core * 2048 + 1024 + t * P + np.arange(P)
        m["kv_idx"] = kvi
        in_maps.append(m)
    return in_maps


_SEQ = ("mamba", "ffn", "attn", "ffn2")
_DBG = False


def kernel(**inputs):
    seq = list(_SEQ)
    in_maps = _prep(inputs, seq)
    nc = build(seq)
    res = run_bass_kernel_spmd(nc, in_maps, core_ids=list(range(NCORES)))
    out = np.empty((4, 2048, D), dtype=np.float32)
    for c in range(NCORES):
        b, h = c // 2, c % 2
        out[b, h * TOK:(h + 1) * TOK, :] = res.results[c]["out"]
    return out


# version 09 (final)
# speedup vs baseline: 1.0264x; 1.0264x over previous
import math
import numpy as np
import ml_dtypes
import concourse.bass as bass
import concourse.mybir as mybir
from concourse.bass_utils import run_bass_kernel_spmd

F32 = mybir.dt.float32
BF16 = mybir.dt.bfloat16
U32 = mybir.dt.uint32
I32 = mybir.dt.int32
AF = mybir.ActivationFunctionType
ALU = mybir.AluOpType
AX = mybir.AxisListType.X

P = 128
TOK = 1024
D = 4096
KT = D // P
NT = TOK // P
NCORES = 8
NW = 512

DIN = 8192
G = 8
NSTATE = 128
CONV_DIM = DIN + 2 * G * NSTATE
NXB = CONV_DIM // P
NH = 128
HD = 64
EPS = 1e-5
NSL_GATE = 16
NSL_XBC = 20
NSL_IP = 36
NEG_BIG = -30000.0

NDMA = 6


class Sched:
    STREAMS = ("pe", "dve", "act", "pool", "sync")

    def __init__(self):
        self.ops = {s: [] for s in self.STREAMS}
        self.ncc = 0
        self.dma_count = {s: 0 for s in self.STREAMS}
        self.last_dma = {s: [None] * NDMA for s in self.STREAMS}

    def op(self, stream, emit, after=()):
        lst = self.ops[stream]
        tok = (stream, len(lst))
        lst.append({"emit": emit, "after": [a for a in after if a is not None],
                    "kind": "op"})
        return tok

    def dma(self, stream, emit, after=()):
        lst = self.ops[stream]
        tok = (stream, len(lst))
        lst.append({"emit": emit, "after": [a for a in after if a is not None],
                    "kind": "dma"})
        slot = self.dma_count[stream] % NDMA
        self.dma_count[stream] += 1
        self.last_dma[stream][slot] = tok
        return tok

    def cc(self, emit, after=()):
        lst = self.ops["pool"]
        tok = ("pool", len(lst))
        lst.append({"emit": emit, "after": [a for a in after if a is not None],
                    "kind": "cc", "ccid": self.ncc})
        self.ncc += 1
        return tok

    def frontier(self):
        out = []
        for s in self.STREAMS:
            if not self.ops[s]:
                continue
            out.append((s, len(self.ops[s]) - 1))
            for tok in self.last_dma[s]:
                if tok is not None:
                    out.append(tok)
        return list(dict.fromkeys(out))

    def finalize(self, nc, block, ctx_stack):
        waited = set()
        for s in self.STREAMS:
            for o in self.ops[s]:
                waited.update(o["after"])
        sem_objs = {}

        def mk_sem(name):
            g = nc.semaphore(name)
            h = g.__enter__()
            ctx_stack.append((name, g))
            return h

        stream_sem = {s: mk_sem("q_" + s) for s in self.STREAMS}
        dma_sems = {s: [mk_sem(f"d_{s}{i}") for i in range(NDMA)]
                    for s in ("act", "pool", "sync", "dve", "pe")}
        cc_sems = [mk_sem(f"cc{i}") for i in range(self.ncc)]

        tokinfo = {}
        for s in self.STREAMS:
            cnt = 0
            dma_i = 0
            dma_seq = [0] * NDMA
            for idx, o in enumerate(self.ops[s]):
                tok = (s, idx)
                if o["kind"] == "dma":
                    slot = dma_i % NDMA
                    dma_seq[slot] += 1
                    o["sem"] = dma_sems[s][slot]
                    o["val"] = 16 * dma_seq[slot]
                    o["inc"] = 16
                    if dma_seq[slot] > 1:
                        o["after"].append(("_dmaslot", s, slot, 16 * (dma_seq[slot] - 1)))
                    dma_i += 1
                    tokinfo[tok] = (o["sem"], o["val"])
                elif o["kind"] == "cc":
                    o["sem"] = cc_sems[o["ccid"]]
                    o["val"] = 1
                    o["inc"] = 1
                    tokinfo[tok] = (o["sem"], 1)
                else:
                    if tok in waited:
                        cnt += 1
                        o["sem"] = stream_sem[s]
                        o["val"] = cnt
                        o["inc"] = 1
                        tokinfo[tok] = (o["sem"], cnt)
                    else:
                        o["sem"] = None
        engmap = {"pe": block.tensor, "dve": block.vector, "act": block.scalar,
                  "pool": block.gpsimd, "sync": block.sync}
        for s in self.STREAMS:
            ops = self.ops[s]
            if not ops:
                continue

            def body(eng, s=s, ops=ops):
                known = {}
                for o in ops:
                    for a in o["after"]:
                        if isinstance(a, tuple) and a and a[0] == "_dmaslot":
                            h, v = dma_sems[a[1]][a[2]], a[3]
                        else:
                            h, v = tokinfo[a]
                        if known.get(id(h), -1) < v:
                            eng.wait_ge(h, v)
                            known[id(h)] = v
                    inst = o["emit"](eng)
                    if o["sem"] is not None:
                        inst.then_inc(o["sem"], o["inc"])
            engmap[s](body)


def _pack_a(WT, nsl, nw=NW):
    K = WT.shape[0]
    kt = K // P
    return np.ascontiguousarray(
        WT.reshape(kt, P, nsl, nw).transpose(2, 1, 0, 3).reshape(nsl, P, kt * nw))


def _pack_b(WT, nm):
    K = WT.shape[0]
    kt = K // P
    return np.ascontiguousarray(
        WT.reshape(kt, P, nm, P).transpose(2, 1, 0, 3).reshape(nm, P, kt * P))



def build(seq):
    nc = bass.Bass("TRN2", target_bir_lowering=False, debug=False)
    sc = Sched()

    x_ext = nc.dram_tensor("x", [TOK, D], F32, kind="ExternalInput").ap()
    out_ext = nc.dram_tensor("out", [TOK, D], F32, kind="ExternalOutput").ap()
    ident_in = nc.dram_tensor("ident", [P, P], F32, kind="ExternalInput").ap()
    ltones_in = nc.dram_tensor("ltones", [P, P], F32, kind="ExternalInput").ap()
    strl_in = nc.dram_tensor("strl", [P, P], F32, kind="ExternalInput").ap()
    maskneg_in = nc.dram_tensor("maskneg", [P, P], F32, kind="ExternalInput").ap()
    expand2_in = nc.dram_tensor("expand2", [8, P], F32, kind="ExternalInput").ap()
    iota8_in = nc.dram_tensor("iota8", [15, 8], F32, kind="ExternalInput").ap()
    emap_in = nc.dram_tensor("emap", [15, 1], I32, kind="ExternalInput").ap()
    negid_in = nc.dram_tensor("negid16", [16, 2048], F32, kind="ExternalInput").ap()

    need_mamba = "mamba" in seq
    need_ffn = ("ffn" in seq) or ("ffn2" in seq)
    need_attn = "attn" in seq

    w_ip = nc.dram_tensor("w_ip", [NSL_IP, P, KT * NW], F32, kind="ExternalInput").ap()
    w_dt = nc.dram_tensor("w_dt", [P, KT * P], F32, kind="ExternalInput").ap()
    w_ob = nc.dram_tensor("w_ob", [32, P, 64 * P], F32, kind="ExternalInput").ap()
    w_base = nc.dram_tensor("w_base", [8, P, KT * NW], F32, kind="ExternalInput").ap()
    w_ra = nc.dram_tensor("w_ra", [P, KT * 143], F32, kind="ExternalInput").ap()
    w_lb = nc.dram_tensor("w_lb", [P, D], F32, kind="ExternalInput").ap()
    w_qkv = nc.dram_tensor("w_qkv", [6, P, KT * NW], F32, kind="ExternalInput").ap()
    w_o2 = nc.dram_tensor("w_o2", [8, P, 8 * NW], F32, kind="ExternalInput").ap()
    convw_in = nc.dram_tensor("convw", [P, NXB * 4], F32, kind="ExternalInput").ap()
    convb_in = nc.dram_tensor("convb", [P, NXB], F32, kind="ExternalInput").ap()
    alog_in = nc.dram_tensor("alog", [1, NH], F32, kind="ExternalInput").ap()
    dvec_in = nc.dram_tensor("dvec", [1, NH], F32, kind="ExternalInput").ap()
    dtb_in = nc.dram_tensor("dtb", [1, NH], F32, kind="ExternalInput").ap()
    normw_in = nc.dram_tensor("normw", [P, 64], F32, kind="ExternalInput").ap()
    baseb_in = nc.dram_tensor("baseb", [1, D], F32, kind="ExternalInput").ap()
    rb_in = nc.dram_tensor("rb", [1, 15], F32, kind="ExternalInput").ap()
    bqkv_in = nc.dram_tensor("bqkv", [1, 3072], F32, kind="ExternalInput").ap()
    bo_in = nc.dram_tensor("bo", [1, D], F32, kind="ExternalInput").ap()
    halo_idx_in = nc.dram_tensor("halo_idx", [3, 1], U32, kind="ExternalInput").ap()
    halo_sel_in = nc.dram_tensor("halo_sel", [3, 1], F32, kind="ExternalInput").ap()
    st_idx_in = nc.dram_tensor("st_idx", [P, 1], U32, kind="ExternalInput").ap()
    st_sel_in = nc.dram_tensor("st_sel", [P, 1], F32, kind="ExternalInput").ap()
    kv_idx_in = nc.dram_tensor("kv_idx", [P, 32], U32, kind="ExternalInput").ap()

    dbg = {}
    if _DBG:
        dbg["xnT"] = nc.dram_tensor("dbg_xnT", [P, KT * TOK], BF16, kind="ExternalOutput").ap()
        dbg["probs"] = nc.dram_tensor("dbg_probs", [P, 16 * NT], F32, kind="ExternalOutput").ap()
        dbg["hw"] = nc.dram_tensor("dbg_hw", [P, NT * P], BF16, kind="ExternalOutput").ap()
        dbg["wTs"] = nc.dram_tensor("dbg_wTs", [8, NT * P], BF16, kind="ExternalOutput").ap()
        dbg["cnt"] = nc.dram_tensor("dbg_cnt", [P, 16 * NT], F32, kind="ExternalOutput").ap()
        dbg["wp16"] = nc.dram_tensor("dbg_wp16", [P, 16 * NT], BF16, kind="ExternalOutput").ap()
        dbg["onehot"] = nc.dram_tensor("dbg_onehot", [15, 8], BF16, kind="ExternalOutput").ap()
        dbg["wpT"] = nc.dram_tensor("dbg_wpT", [15, NT * P], BF16, kind="ExternalOutput").ap()
    n_bound = len(seq) - 1
    streams_d = [nc.dram_tensor(f"s{i+1}", [NT, 8, P, NW], F32).ap() for i in range(n_bound)]
    if need_mamba:
        z_dram = nc.dram_tensor("z_dram", [NSL_IP, TOK, NW], BF16).ap()
        xt_dram = nc.dram_tensor("xt_dram", [NT, P, DIN], BF16).ap()
        snap_dram = nc.dram_tensor("snap_dram", [NT, P, DIN], BF16).ap()
        yt_dram = nc.dram_tensor("yt_dram", [NT, 64, P, P], BF16).ap()
        acst_dram = nc.dram_tensor("acst_dram", [4, P, NT * P], BF16).ap()
        cc_halo_in = nc.dram_tensor("cc_halo_in", [3, CONV_DIM], BF16).ap()
        cc_halo_out = nc.dram_tensor("cc_halo_out", [NCORES * 3, CONV_DIM], BF16).ap()
        cc_st_in = nc.dram_tensor("cc_st_in", [P, DIN], F32).ap()
        cc_st_out = nc.dram_tensor("cc_st_out", [NCORES * P, DIN], F32).ap()
    if need_attn:
        cc_kv_in = nc.dram_tensor("cc_kv_in", [2048, 1024], BF16).ap()
        cc_kv_out = nc.dram_tensor("cc_kv_out", [NCORES * 2048, 1024], BF16).ap()

    ctx_stack = []
    _uid = [0]

    def sb(name, shape, dtype):
        _uid[0] += 1
        name = f"{name}_u{_uid[0]}"
        g = nc.sbuf_tensor(name, shape, dtype)
        t = g.__enter__()
        ctx_stack.append((name, g))
        return t

    def ps(name, shape, dtype):
        _uid[0] += 1
        name = f"{name}_u{_uid[0]}"
        g = nc.psum_tensor(name, shape, dtype)
        t = g.__enter__()
        ctx_stack.append((name, g))
        return t

    def mark():
        return len(ctx_stack)

    def pop_to(m):
        while len(ctx_stack) > m:
            ctx_stack.pop()[1].__exit__(None, None, None)

    idn = sb("idn", [P, P], BF16)
    idnf = sb("idnf", [P, P], F32)
    ones_row = sb("ones_row", [1, P], BF16)
    mones_row = sb("mones_row", [1, P], BF16)
    cvals = sb("cvals", [P, 2], F32)

    t_idn = sc.dma("pool", lambda g: g.dma_start(out=idn[:], in_=ident_in))
    t_idnf = sc.dma("sync", lambda g: g.dma_start(out=idnf[:], in_=ident_in))
    t_ones = sc.op("pool", lambda g: g.memset(ones_row[:], 1.0))
    t_mones = sc.op("pool", lambda g: g.memset(mones_row[:], -1.0))
    t_c0 = sc.op("pool", lambda g: g.memset(cvals[:, 0:1], EPS))
    t_c1 = sc.op("pool", lambda g: g.memset(cvals[:, 1:2], 1.0))
    nc.const_aps.register(F32, EPS, cvals[:, 0:1])
    nc.const_aps.register(F32, 1.0, cvals[:, 1:2])

    def emit_ln(src_dram, src_rowmajor, ln_bufs, xnT, deps):
        xtile, xnt_b, cols, ptr = ln_bufs
        t_norm = [None] * NT
        tr_last = [None] * NT
        t_evac = {}
        for t in range(NT):
            af = list(deps) + ([t_norm[t - 2]] if t >= 2 else [])
            if src_rowmajor:
                t_ld = sc.dma("sync", lambda g, t=t: g.dma_start(
                    out=xtile[t % 2][:], in_=src_dram[t * P:(t + 1) * P, :]), after=af)
            else:
                t_ld = sc.dma("sync", lambda g, t=t: g.dma_start(
                    out=xtile[t % 2][:].rearrange("p (a b) -> p a b", a=8),
                    in_=src_dram[t].rearrange("a p b -> p a b")), after=af)
            xt = xtile[t % 2]
            co = cols[:, t * 8:(t + 1) * 8]
            a2 = sc.op("act", lambda s, t=t, xt=xt, co=co: s.activation(
                xnt_b[t % 2][:], xt[:], AF.Square, accum_out=co[:, 1:2]),
                after=[t_ld] + ([tr_last[t - 2]] if t >= 2 else []))
            a1 = sc.op("dve", lambda v, xt=xt, co=co: v.tensor_reduce(
                co[:, 0:1], xt[:], AX, ALU.add), after=[t_ld])
            a3 = sc.op("dve", lambda v, co=co: v.tensor_scalar(
                co[:, 2:4], co[:, 0:2], 1.0 / D, None, ALU.mult), after=[a1, a2])
            a4 = sc.op("dve", lambda v, co=co: v.tensor_tensor(
                co[:, 4:5], co[:, 2:3], co[:, 2:3], ALU.mult), after=[a3])
            a5 = sc.op("dve", lambda v, co=co: v.tensor_tensor(
                co[:, 5:6], co[:, 3:4], co[:, 4:5], ALU.subtract), after=[a4])
            a6 = sc.op("act", lambda s, co=co: s.activation(
                co[:, 6:7], co[:, 5:6], AF.Ln, bias=EPS, scale=1.0), after=[a5])
            a7 = sc.op("act", lambda s, co=co: s.activation(
                co[:, 7:8], co[:, 6:7], AF.Exp, scale=-0.5), after=[a6])
            afn = [a7, a2] + ([tr_last[t - 2]] if t >= 2 else [])
            t_norm[t] = sc.op("dve", lambda v, t=t, xt=xt, co=co: v.tensor_scalar(
                xnt_b[t % 2][:], xt[:], co[:, 2:3], co[:, 7:8],
                ALU.subtract, ALU.mult), after=afn)
            for kt in range(KT):
                i = t * KT + kt
                af2 = [t_norm[t], t_idn]
                if i >= 4:
                    af2.append(t_evac[i - 4])
                tr = sc.op("pe", lambda t_, t=t, kt=kt, i=i: t_.transpose(
                    ptr[i % 4][:], xnt_b[t % 2][:, kt * P:(kt + 1) * P], idn[:]), after=af2)
                t_evac[i] = sc.op("act", lambda s, t=t, kt=kt, i=i: s.copy(
                    xnT[:, kt * TOK + t * P:kt * TOK + (t + 1) * P], ptr[i % 4][:]), after=[tr])
            tr_last[t] = t_evac[t * KT + KT - 1]
        all_done = [t_evac[i] for i in range(NT * KT)]
        return all_done, t_norm

    def mk_ln_bufs(pfx):
        xtile = [sb(f"{pfx}_x{i}", [P, D], F32) for i in range(2)]
        xnt_b = [sb(f"{pfx}_n{i}", [P, D], BF16) for i in range(2)]
        cols = sb(f"{pfx}_cols", [P, 8 * NT], F32)
        ptr = [ps(f"{pfx}_tr{i}", [P, P], BF16) for i in range(4)]
        return (xtile, xnt_b, cols, ptr)


    def emit_ffn(src_dram, src_rm, dst_dram, dst_rm, deps):
        m0 = mark()
        xnT = sb("xnT_f", [P, KT * TOK], BF16)
        mL = mark()
        ln_bufs = mk_ln_bufs("fln")
        xn_done, t_norm = emit_ln(src_dram, src_rm, ln_bufs, xnT, deps)
        pop_to(mL)
        deps = list(deps) + [xn_done[-1], t_norm[-1]]

        wra = sb("wra", [P, KT * 143], BF16)
        wlb = sb("wlb", [P, D], BF16)
        bb_row = sb("bb_row", [1, D], BF16)
        rb_row = sb("rb_row", [1, 15], BF16)
        emap_i = sb("emap_i", [15, 1], I32)
        emap_f = sb("emap_f", [15, 1], F32)
        iota8 = sb("sb_iota8", [15, 8], F32)
        onehot = sb("onehot", [15, 8], BF16)
        expand2 = sb("sb_expand2", [8, P], BF16)
        hw_all = sb("hw_all", [P, NT * P], BF16)
        hbf = sb("hbf", [P, NT * P], BF16)
        pbuf = sb("pbuf", [P, 16 * NT], F32)
        cnt = sb("cnt", [P, 16 * NT], F32)
        wp16 = sb("wp16", [P, 16 * NT], BF16)
        wpT = sb("wpT", [15, NT * P], BF16)
        wTs = sb("wTs", [8, NT * P], BF16)
        scrf = sb("scrf", [P, 4 * NT], F32)
        wbuf = [sb(f"fw{i}", [P, 16 * NW], BF16) for i in range(3)]
        stage = [sb(f"fs{i}", [P, NW], F32) for i in range(4)]
        resb = [sb(f"fr{i}", [P, NW], F32) for i in range(4)]
        pra = [ps(f"fpra{i}", [P, 160], F32) for i in range(2)]
        ptr2 = ps("fptr2", [P, P], BF16)
        pw2 = ps("fpw2", [P, P], F32)
        pmm = [ps(f"fpm{i}", [P, NW], F32) for i in range(4)]

        t_wra = sc.dma("pool", lambda g: g.dma_start(out=wra[:], in_=w_ra), after=deps)
        t_wlb = sc.dma("pool", lambda g: g.dma_start(out=wlb[:], in_=w_lb), after=deps)
        t_bb = sc.dma("pool", lambda g: g.dma_start(out=bb_row[:], in_=baseb_in), after=deps)
        t_rb = sc.dma("pool", lambda g: g.dma_start(out=rb_row[:], in_=rb_in), after=deps)
        t_em = sc.dma("sync", lambda g: g.dma_start(out=emap_i[:], in_=emap_in), after=deps)
        t_io = sc.dma("sync", lambda g: g.dma_start(out=iota8[:], in_=iota8_in), after=deps)
        t_e2 = sc.dma("pool", lambda g: g.dma_start(out=expand2[:], in_=expand2_in), after=deps)
        t_pz = sc.op("pool", lambda g: g.memset(pbuf[:], 0.0), after=deps)
        t_cz = sc.op("pool", lambda g: g.memset(cnt[:], 0.0), after=deps)
        t_wz = sc.op("pool", lambda g: g.memset(wp16[:], 0.0), after=deps)
        t_emf = sc.op("dve", lambda v: v.tensor_copy(emap_f[:], emap_i[:]), after=[t_em])
        t_oh = sc.op("dve", lambda v: v.tensor_scalar(
            onehot[:], iota8[:], emap_f[:, 0:1], None, ALU.is_equal), after=[t_emf, t_io])

        t_hw = [None] * NT
        prev_pra_use = [[], []]
        prev_ptr2_use = None
        prev_pw2_use = None
        for t in range(NT):
            af = [t_wra] + deps + prev_pra_use[t % 2]
            mm = None
            for kt in range(KT):
                mm = sc.op("pe", lambda t_, t=t, kt=kt: t_.matmul(
                    pra[t % 2][:, 0:143],
                    xnT[:, kt * TOK + t * P:kt * TOK + (t + 1) * P],
                    wra[:, kt * 143:(kt + 1) * 143],
                    start=(kt == 0), stop=False),
                    after=(af if kt == 0 else []))
            mmb = sc.op("pe", lambda t_, t=t: t_.matmul(
                pra[t % 2][:, P:P + 15], ones_row[0:1, :], rb_row[0:1, :],
                start=False, stop=True), after=[mm, t_rb, t_ones])
            co = scrf[:, 4 * t:4 * t + 4]
            pb = pbuf[:, 16 * t:16 * t + 15]
            cn = cnt[:, 16 * t:16 * t + 15]
            wp = wp16[:, 16 * t:16 * t + 15]
            s1 = sc.op("dve", lambda v, t=t, co=co: v.reduce_max(
                co[:, 0:1], pra[t % 2][:, P:P + 15], AX, negate=True), after=[mmb])
            s2 = sc.op("act", lambda s, t=t, co=co, pb=pb: s.activation(
                pb, pra[t % 2][:, P:P + 15], AF.Exp,
                bias=co[:, 0:1], scale=1.0, accum_out=co[:, 1:2]), after=[s1, t_pz])
            s3 = sc.op("dve", lambda v, co=co: v.reciprocal(co[:, 2:3], co[:, 1:2]), after=[s2])
            s4 = sc.op("dve", lambda v, co=co, pb=pb: v.tensor_scalar(
                pb, pb, co[:, 2:3], None, ALU.mult), after=[s3])
            s5 = sc.op("dve", lambda v, pb=pb, cn=cn: v.tensor_scalar(
                cn, pb, pb[:, 0:1], None, ALU.is_lt), after=[s4, t_cz])
            for j in range(1, 15):
                s5 = sc.op("dve", lambda v, j=j, pb=pb, cn=cn: v.scalar_tensor_tensor(
                    cn, pb, pb[:, j:j + 1], cn, ALU.is_lt, ALU.add), after=[s5])
            s6 = sc.op("dve", lambda v, cn=cn: v.tensor_scalar(
                cn, cn, 7.5, None, ALU.is_lt), after=[s5])
            s7 = sc.op("dve", lambda v, pb=pb, cn=cn, wp=wp: v.tensor_tensor(
                wp, pb, cn, ALU.mult), after=[s6, t_wz])
            af8 = [s7, t_idn] + ([prev_ptr2_use] if prev_ptr2_use else [])
            s8 = sc.op("pe", lambda t_, wp=wp: t_.transpose(
                ptr2[0:15, 0:P], wp, idn[:]), after=af8)
            s8e = sc.op("act", lambda s, t=t: s.copy(
                wpT[:, t * P:(t + 1) * P], ptr2[0:15, 0:P]), after=[s8])
            af9 = [s8e, t_oh] + ([prev_pw2_use] if prev_pw2_use else [])
            s9 = sc.op("pe", lambda t_, t=t: t_.matmul(
                pw2[0:8, 0:P], onehot[:, 0:8], wpT[:, t * P:(t + 1) * P],
                start=True, stop=True), after=af9)
            s9e = sc.op("act", lambda s, t=t: s.copy(
                wTs[:, t * P:(t + 1) * P], pw2[0:8, 0:P]), after=[s9])
            s10 = sc.op("pe", lambda t_, t=t: t_.matmul(
                pw2[:, 0:P], wTs[0:8, t * P:(t + 1) * P], expand2[0:8, :],
                start=True, stop=True), after=[s9e, t_e2])
            s11 = sc.op("act", lambda s, t=t: s.copy(
                hbf[:, t * P:(t + 1) * P], pra[t % 2][:, 0:P]), after=[mmb])
            s12 = sc.op("dve", lambda v, t=t: v.tensor_tensor(
                hbf[:, t * P:(t + 1) * P], hbf[:, t * P:(t + 1) * P],
                pw2[:, 0:P], ALU.mult), after=[s10, s11])
            prev_pw2_use = s12
            prev_pra_use[t % 2] = [s2, s11]
            af13 = [s12, s8e]
            s13 = sc.op("pe", lambda t_, t=t: t_.transpose(
                ptr2[:, 0:P], hbf[:, t * P:(t + 1) * P], idn[:]), after=af13)
            prev_ptr2_use = sc.op("act", lambda s, t=t: s.copy(
                hw_all[:, t * P:(t + 1) * P], ptr2[:, 0:P]), after=[s13])
            t_hw[t] = prev_ptr2_use

        t_store = []
        buf_use = [None, None, None]
        prev_stage_use = {}
        for s_i in range(8):
            t_wh = [None, None]
            for hf in range(2):
                bidx = (2 * s_i + hf) % 3
                af = list(deps)
                if buf_use[bidx] is not None:
                    af.append(buf_use[bidx])
                t_wh[hf] = sc.dma("pool", lambda g, s_i=s_i, hf=hf, bidx=bidx: g.dma_start(
                    out=wbuf[bidx][:], in_=w_base[s_i][:, hf * 16 * NW:(hf + 1) * 16 * NW]),
                    after=af)
            mmlast = None
            for t in range(NT):
                gi = s_i * NT + t
                pb = pmm[gi % 4]
                af2 = [t_wh[0]] + deps
                if gi >= 4:
                    af2.append(prev_stage_use[gi % 4])
                for kt in range(KT):
                    hf = kt // 16
                    bidx = (2 * s_i + hf) % 3
                    mmlast = sc.op("pe", lambda t_, s_i=s_i, t=t, kt=kt, pb=pb, bidx=bidx: t_.matmul(
                        pb[:], xnT[:, kt * TOK + t * P:kt * TOK + (t + 1) * P],
                        wbuf[bidx][:, (kt % 16) * NW:(kt % 16 + 1) * NW],
                        start=(kt == 0), stop=False),
                        after=(af2 if kt == 0 else ([t_wh[1]] if kt == 16 else [])))
                    if t == NT - 1 and kt % 16 == 15:
                        buf_use[bidx] = mmlast
                mb = sc.op("pe", lambda t_, s_i=s_i, pb=pb: t_.matmul(
                    pb[:], ones_row[0:1, :], bb_row[0:1, s_i * NW:(s_i + 1) * NW],
                    start=False, stop=False), after=[mmlast, t_bb])
                ml = sc.op("pe", lambda t_, s_i=s_i, t=t, pb=pb: t_.matmul(
                    pb[:], hw_all[:, t * P:(t + 1) * P], wlb[:, s_i * NW:(s_i + 1) * NW],
                    start=False, stop=True), after=[mb, t_hw[t], t_wlb])
                rl = sc.dma("sync", lambda g, s_i=s_i, t=t, gi=gi: g.dma_start(
                    out=resb[gi % 4][:],
                    in_=(src_dram[t * P:(t + 1) * P, s_i * NW:(s_i + 1) * NW] if src_rm
                         else src_dram[t, s_i])),
                    after=(deps if gi < 4 else [prev_stage_use[gi % 4]]))
                ev = sc.op("dve", lambda v, gi=gi, pb=pb: v.tensor_tensor(
                    stage[gi % 4][:], pb[:], resb[gi % 4][:], ALU.add), after=[ml, rl])
                st = sc.dma("sync", lambda g, s_i=s_i, t=t, gi=gi: g.dma_start(
                    out=(dst_dram[t * P:(t + 1) * P, s_i * NW:(s_i + 1) * NW] if dst_rm
                         else dst_dram[t, s_i]), in_=stage[gi % 4][:]), after=[ev])
                prev_stage_use[gi % 4] = st
                t_store.append(st)
        if _DBG:
            sc.dma("sync", lambda g: g.dma_start(out=dbg["xnT"], in_=xnT[:]), after=[xn_done[-1]])
            sc.dma("sync", lambda g: g.dma_start(out=dbg["probs"], in_=pbuf[:]), after=t_hw)
            sc.dma("sync", lambda g: g.dma_start(out=dbg["hw"], in_=hw_all[:]), after=t_hw)
            sc.dma("sync", lambda g: g.dma_start(out=dbg["wTs"], in_=wTs[:]), after=t_hw)
            sc.dma("sync", lambda g: g.dma_start(out=dbg["cnt"], in_=cnt[:]), after=t_hw)
            sc.dma("sync", lambda g: g.dma_start(out=dbg["wp16"], in_=wp16[:]), after=t_hw)
            sc.dma("sync", lambda g: g.dma_start(out=dbg["onehot"], in_=onehot[:]), after=t_hw)
            sc.dma("sync", lambda g: g.dma_start(out=dbg["wpT"], in_=wpT[:]), after=t_hw)
        fr = sc.frontier()
        pop_to(m0)
        return fr

    def emit_mamba(src_dram, src_rm, dst_dram, dst_rm, deps):
        m0 = mark()
        abc = sb("abc", [P, P], F32)
        abc8 = sb("abc8", [P, NT * P], F32)
        dbc = sb("dbc", [P, P], BF16)
        ltones = sb("ltones_b", [P, P], BF16)
        strl = sb("strl_b", [P, P], BF16)
        mask4 = sb("mask4", [P, 4 * P], BF16)
        convw = sb("convw_b", [P, NXB * 4], F32)
        convb = sb("convb_b", [P, NXB], F32)
        rows = sb("m_rows", [1, 2 * P], F32)
        rowsb = sb("m_rowsb", [1, 2 * P], BF16)
        t_lt = sc.dma("pool", lambda g: g.dma_start(out=ltones[:], in_=ltones_in), after=deps)
        t_sl = sc.dma("pool", lambda g: g.dma_start(out=strl[:], in_=strl_in), after=deps)
        t_m4 = sc.dma("pool", lambda g: g.dma_start(out=mask4[:, 0:P], in_=maskneg_in), after=deps)
        t_m4b = sc.op("dve", lambda v: v.tensor_copy(mask4[:, P:2 * P], mask4[:, 0:P]), after=[t_m4])
        t_m4c = sc.op("dve", lambda v: v.tensor_copy(mask4[:, 2 * P:4 * P], mask4[:, 0:2 * P]),
                      after=[t_m4b])
        t_cw = sc.dma("pool", lambda g: g.dma_start(out=convw[:], in_=convw_in), after=deps)
        t_cb = sc.dma("pool", lambda g: g.dma_start(out=convb[:], in_=convb_in), after=deps)
        t_al = sc.dma("sync", lambda g: g.dma_start(out=rows[0:1, 0:P], in_=alog_in), after=deps)
        t_dv = sc.dma("sync", lambda g: g.dma_start(out=rows[0:1, P:2 * P], in_=dvec_in), after=deps)
        a1 = sc.op("act", lambda s: s.activation(rows[0:1, 0:P], rows[0:1, 0:P], AF.Exp, scale=1.0),
                   after=[t_al])
        a2 = sc.op("dve", lambda v: v.tensor_scalar(rows[0:1, 0:P], rows[0:1, 0:P], -1.0, None,
                                                    ALU.mult), after=[a1])
        a3 = sc.op("act", lambda s: s.copy(rowsb[0:1, 0:P], rows[0:1, 0:P]), after=[a2])
        a4 = sc.op("act", lambda s: s.copy(rowsb[0:1, P:2 * P], rows[0:1, P:2 * P]), after=[t_dv])

        mS = mark()
        sdv = sb("sdv", [P, NT * P], F32)
        dsx = sb("dsx", [P, NT * P], F32)
        lam_all = sb("lam_all", [P, NT * P], F32)
        dtraw = lam_all
        nwc = sb("nwc", [P, 64], F32)
        BCT = sb("BCT", [P, NT * 2048], BF16)

        mpre = mark()
        pcon = ps("pcon", [P, NW], F32)
        t_nw = sc.dma("sync", lambda g: g.dma_start(out=nwc[:], in_=normw_in), after=deps)
        b1 = sc.op("pe", lambda t_: t_.matmul(pcon[:, 0:P], ones_row[0:1, :], rowsb[0:1, 0:P],
                                              start=True, stop=True), after=[a3, t_ones])
        b2 = sc.op("dve", lambda v: v.tensor_copy(abc[:], pcon[:, 0:P]), after=[b1])
        ab8 = []
        pb8 = b2
        for cc8 in range(NT):
            pb8 = sc.op("dve", lambda v, cc8=cc8: v.tensor_copy(
                abc8[:, cc8 * P:(cc8 + 1) * P], abc[:]), after=[pb8])
            ab8.append(pb8)
        b3 = sc.op("pe", lambda t_: t_.matmul(pcon[:, 0:P], ones_row[0:1, :], rowsb[0:1, P:2 * P],
                                              start=True, stop=True), after=[b2, a4])
        b4 = sc.op("dve", lambda v: v.tensor_copy(dbc[:], pcon[:, 0:P]), after=[b3])
        pop_to(mpre)
        deps_m = list(deps) + [t_m4c, t_cw, t_cb, b2, b4, t_nw]

        mX = mark()
        xnT = sb("xnT_m", [P, KT * TOK], BF16)
        mL = mark()
        ln_bufs = mk_ln_bufs("mln")
        xn_done, t_norm = emit_ln(src_dram, src_rm, ln_bufs, xnT, deps)
        pop_to(mL)
        depsA = list(deps) + [xn_done[-1], t_norm[-1]]
        wib = [sb(f"wib{i}", [P, 16 * NW], BF16) for i in range(3)]
        wdtb = sb("wdtb", [P, KT * P], BF16)
        dtb_row = sb("dtb_row", [1, NH], BF16)
        zstage = [sb(f"zst{i}", [P, NW], BF16) for i in range(4)]
        pz = [ps(f"pz{i}", [P, NW], F32) for i in range(4)]
        t_wdt = sc.dma("pool", lambda g: g.dma_start(out=wdtb[:], in_=w_dt), after=depsA)
        t_dtb = sc.dma("pool", lambda g: g.dma_start(out=dtb_row[:], in_=dtb_in), after=depsA)
        z_done = []
        buf_use = [None, None, None]
        prev_bank = {}
        for s_i in range(NSL_IP):
            t_wh = [None, None]
            for hf in range(2):
                bidx = (2 * s_i + hf) % 3
                af = list(depsA)
                if buf_use[bidx] is not None:
                    af.append(buf_use[bidx])
                t_wh[hf] = sc.dma("pool", lambda g, s_i=s_i, hf=hf, bidx=bidx: g.dma_start(
                    out=wib[bidx][:], in_=w_ip[s_i][:, hf * 16 * NW:(hf + 1) * 16 * NW]), after=af)
            for t in range(NT):
                gi = s_i * NT + t
                pb = pz[gi % 4]
                af2 = [t_wh[0]] + depsA
                if gi >= 4:
                    af2.append(prev_bank[gi % 4])
                mml = None
                for kt in range(KT):
                    hf = kt // 16
                    bidx = (2 * s_i + hf) % 3
                    mml = sc.op("pe", lambda t_, t=t, kt=kt, pb=pb, bidx=bidx: t_.matmul(
                        pb[:], xnT[:, kt * TOK + t * P:kt * TOK + (t + 1) * P],
                        wib[bidx][:, (kt % 16) * NW:(kt % 16 + 1) * NW],
                        start=(kt == 0), stop=(kt == KT - 1)),
                        after=(af2 if kt == 0 else ([t_wh[1]] if kt == 16 else [])))
                    if t == NT - 1 and kt % 16 == 15:
                        buf_use[bidx] = mml
                ev = sc.op("act", lambda s, gi=gi, pb=pb: s.copy(zstage[gi % 4][:], pb[:]),
                           after=[mml])
                st = sc.dma("sync", lambda g, s_i=s_i, t=t, gi=gi: g.dma_start(
                    out=z_dram[s_i][t * P:(t + 1) * P, :], in_=zstage[gi % 4][:]), after=[ev])
                prev_bank[gi % 4] = st
                z_done.append(st)
        dt_done = []
        for t in range(NT):
            pb = pz[t % 4]
            af2 = [t_wdt, prev_bank[t % 4]]
            mml = None
            for kt in range(KT):
                mml = sc.op("pe", lambda t_, t=t, kt=kt, pb=pb: t_.matmul(
                    pb[:, 0:P], xnT[:, kt * TOK + t * P:kt * TOK + (t + 1) * P],
                    wdtb[:, kt * P:(kt + 1) * P],
                    start=(kt == 0), stop=False), after=(af2 if kt == 0 else [])) 
            mb = sc.op("pe", lambda t_, pb=pb: t_.matmul(
                pb[:, 0:P], ones_row[0:1, :], dtb_row[0:1, :],
                start=False, stop=True), after=[mml, t_dtb, t_ones])
            dte = sc.op("dve", lambda v, t=t, pb=pb: v.tensor_copy(
                dtraw[:, t * P:(t + 1) * P], pb[:, 0:P]), after=[mb])
            prev_bank[t % 4] = dte
            dt_done.append(dte)
        t_hc = sc.dma("sync", lambda g: g.dma_start(
            out=cc_halo_in[:].rearrange("r (s w) -> r s w", s=NSL_XBC),
            in_=z_dram[NSL_GATE:NSL_IP, TOK - 3:TOK, :].rearrange("s r w -> r s w")),
            after=z_done[-1:] + z_done[::37])
        af_cc = [t_hc] + cc_chain
        t_cch = sc.cc(lambda g: g.collective_compute(
            "AllGather", ALU.bypass, replica_groups=[list(range(NCORES))],
            ins=[cc_halo_in[:]], outs=[cc_halo_out[:]]), after=af_cc)
        cc_chain.append(t_cch)
        m1_fin = [z_done[-1], dt_done[-1]]
        pop_to(mX)

        mA2 = mark()
        cum = sb("cum", [P, DIN], F32)
        xbct = sb("xbct", [P, NXB * 131], BF16)
        sidx = sb("sidx", [P, 1], U32)
        t_si = sc.dma("sync", lambda g: g.dma_start(out=sidx[:], in_=st_idx_in),
                      after=deps_m + m1_fin)
        xbct_v = xbct[:].rearrange("p (m w) -> p m w", w=131)

        mPP = mark()
        dtv = sb("dtv", [P, NT * P], F32)
        lnd = sb("lnd", [P, NT * P], F32)
        tmpa = sb("tmpa", [P, NT * P], F32)
        tmpb = sb("tmpb", [P, NT * P], F32)
        acs_all = sb("acs_all", [P, NT * P], F32)
        acsm_all = sb("acsm_all", [P, NT * P], F32)
        adt_all = sb("adt_all", [P, NT * P], BF16)
        acsT = [sb(f"acsT{i}", [P, NT * P], F32) for i in range(2)]
        hlb = sb("hlb", [P, NT * P], BF16)
        hlf = sb("hlf", [P, NT * P], F32)
        halosb = sb("halosb", [3, CONV_DIM], BF16)
        hsel = sb("hsel", [3, 1], F32)
        hidx = sb("hidx", [3, 1], U32)
        pp1 = ps("pp1", [P, NT * P], F32)
        pp2 = ps("pp2", [P, NT * P], F32)
        pptr = [ps(f"pptr{i}", [P, P], F32) for i in range(2)]
        pht = [ps(f"pht{i}", [P, 4], BF16) for i in range(2)]

        m1f = deps_m + m1_fin
        t_cz = sc.op("pool", lambda g: g.memset(cum[:], 0.0), after=m1f)
        d1 = sc.op("act", lambda s: s.activation(tmpa[:], dtraw[:], AF.Abs, scale=1.0),
                   after=m1f + [dt_done[-1]])
        d2 = sc.op("act", lambda s: s.activation(tmpb[:], tmpa[:], AF.Exp, scale=-1.0), after=[d1])
        d3 = sc.op("act", lambda s: s.activation(tmpb[:], tmpb[:], AF.Ln, bias=1.0, scale=1.0),
                   after=[d2])
        d4 = sc.op("dve", lambda v: v.tensor_scalar(tmpa[:], dtraw[:], 0.0, None, ALU.max),
                   after=[d2] + m1f)
        d5 = sc.op("dve", lambda v: v.tensor_tensor(dtv[:], tmpa[:], tmpb[:], ALU.add),
                   after=[d3, d4])
        d6 = sc.op("act", lambda s: s.activation(lnd[:], dtv[:], AF.Ln, scale=1.0), after=[d5])
        d7 = sc.op("dve", lambda v: v.tensor_tensor(adt_all[:], dtv[:], abc8[:], ALU.mult),
                   after=[d5] + ab8[-1:])
        macs = [sc.op("pe", lambda t_, i=i: t_.matmul(
            pp1[:, i * NW:(i + 1) * NW], ltones[:], adt_all[:, i * NW:(i + 1) * NW],
            start=True, stop=True), after=[d7, t_lt] + m1f) for i in range(2)]
        mrev = [sc.op("pe", lambda t_, i=i: t_.matmul(
            pp2[:, i * NW:(i + 1) * NW], strl[:], adt_all[:, i * NW:(i + 1) * NW],
            start=True, stop=True), after=[d7, t_sl] + macs) for i in range(2)]
        ea = sc.op("dve", lambda v: v.tensor_copy(acs_all[:], pp1[:]), after=macs)
        es = sc.op("act", lambda s: s.activation(sdv[:], pp1[:], AF.Exp, scale=1.0),
                   after=macs + [ea])
        er = sc.op("dve", lambda v: v.tensor_tensor(tmpa[:], pp2[:], lnd[:], ALU.add),
                   after=mrev + [d6, d5])
        ed = sc.op("act", lambda s: s.activation(dsx[:], tmpa[:], AF.Exp, scale=1.0), after=[er])
        em = sc.op("dve", lambda v: v.tensor_tensor(acsm_all[:], acs_all[:], lnd[:],
                                                    ALU.subtract), after=[ea, d6])
        e6 = sc.op("act", lambda s: s.copy(rows[0:1, 0:P] if False else tmpb[0:1, 0:NT * P],
                                           pp2[0:1, :]), after=mrev + [er])
        e7 = sc.op("dve", lambda v: v.tensor_tensor(
            tmpb[0:1, 0:NT * P], acs_all[0:1, :], tmpb[0:1, 0:NT * P], ALU.add),
            after=[e6, ea])
        e8 = sc.op("act", lambda s: s.activation(
            tmpb[0:1, 0:NT * P], tmpb[0:1, 0:NT * P], AF.Exp, scale=1.0), after=[e7])
        e9 = sc.op("act", lambda s: s.copy(hlb[0:1, 0:NT * P], tmpb[0:1, 0:NT * P]), after=[e8])
        ml8 = []
        for i in range(2):
            mm8 = sc.op("pe", lambda t_, i=i: t_.matmul(
                pp1[:, i * NW:(i + 1) * NW], ones_row[0:1, :], hlb[0:1, i * NW:(i + 1) * NW],
                start=True, stop=True), after=[e9, es, t_ones])
            ml8.append(sc.op("dve", lambda v, i=i: v.tensor_copy(
                lam_all[:, i * NW:(i + 1) * NW], pp1[:, i * NW:(i + 1) * NW]),
                after=[mm8, d1, d4, d7]))
        acst_w = []
        evs = []
        for pl in range(2):
            srcb = [acs_all, acsm_all][pl]
            for c8 in range(NT):
                trp = sc.op("pe", lambda t_, pl=pl, c8=c8, srcb=srcb: t_.transpose(
                    pptr[c8 % 2][:], srcb[:, c8 * P:(c8 + 1) * P], idnf[:]),
                    after=[em, ea, t_idnf, ml8[-1]] + ([evs[-2]] if len(evs) >= 2 else []))
                evs.append(sc.op("dve", lambda v, pl=pl, c8=c8: v.tensor_copy(
                    acsT[pl][:, c8 * P:(c8 + 1) * P], pptr[c8 % 2][:]), after=[trp]))
            h1 = sc.op("dve", lambda v, pl=pl: v.tensor_copy(hlb[:], acsT[pl][:]),
                       after=evs[pl * NT:(pl + 1) * NT] + ([acst_w[-1]] if acst_w else [e9]))
            wh = sc.dma("sync", lambda g, pl=pl: g.dma_start(
                out=acst_dram[2 * pl], in_=hlb[:]), after=[h1])
            h2 = sc.op("dve", lambda v, pl=pl: v.tensor_copy(hlf[:], hlb[:]), after=[h1])
            h3 = sc.op("dve", lambda v, pl=pl: v.tensor_tensor(
                hlf[:], acsT[pl][:], hlf[:], ALU.subtract), after=[h2])
            h4 = sc.op("dve", lambda v, pl=pl: v.tensor_copy(hlb[:], hlf[:]), after=[h3, wh])
            wl = sc.dma("sync", lambda g, pl=pl: g.dma_start(
                out=acst_dram[2 * pl + 1], in_=hlb[:]), after=[h4])
            acst_w += [wh, wl]
        t_hi = sc.dma("sync", lambda g: g.dma_start(out=hidx[:], in_=halo_idx_in), after=m1f)
        t_hs = sc.dma("sync", lambda g: g.dma_start(out=hsel[:], in_=halo_sel_in), after=m1f)
        t_hg = sc.dma("pool", lambda g: g.indirect_dma_start(
            halosb[:], None, cc_halo_out[:], bass.IndirectOffsetOnAxis(hidx[:, 0:1], 0)),
            after=[t_cch, t_hi])
        t_hm = sc.op("dve", lambda v: v.tensor_scalar(
            halosb[:], halosb[:], hsel[:, 0:1], None, ALU.mult), after=[t_hg, t_hs])
        t_carry = []
        pc_last = [None, None]
        for m in range(NXB):
            af = [t_hm, t_idn] + m1f + ([pc_last[m % 2]] if pc_last[m % 2] else [])
            trh = sc.op("pe", lambda t_, m=m: t_.transpose(
                pht[m % 2][0:P, 0:3], halosb[0:3, m * P:(m + 1) * P], idn[0:3, 0:3]), after=af)
            tre = sc.op("act", lambda s, m=m: s.copy(
                xbct[:, m * 131:m * 131 + 3], pht[m % 2][0:P, 0:3]), after=[trh])
            pc_last[m % 2] = tre
            t_carry.append(tre)
        pp_fin = sc.frontier()
        pop_to(mPP)

        zx = sb("zx", [P, NSL_XBC * NW], BF16)
        ctm = [[sb(f"ctm{e}{i}", [P, P], F32) for i in range(2)] for e in range(2)]
        ctp2 = sb("ctp2", [P, P], F32)
        xbcp = sb("xbcp", [P, NXB * P], BF16)
        xtok = sb("xtok", [P, DIN], BF16)
        xs2g = sb("xs2g", [P, 1024], BF16)
        btokc = sb("btokc", [P, P], BF16)
        p_tr = [ps(f"mp_tr{i}", [P, 4 * P], BF16) for i in range(2)]
        p_st = ps("mp_st", [P, 1024], F32)

        st_final = None
        cum_last = {}
        prev_zx = None
        prev_sil_q = [None] * 4
        prev_xtr_q = [None] * 4
        prev_xt_dma = None
        prev_stg = None
        xbcp_v = xbcp[:].rearrange("p (m q) -> p m q", q=P)
        snap_w = []
        for c in range(NT):
            t_snap = sc.dma("pool", lambda g, c=c: g.dma_start(
                out=snap_dram[c], in_=cum[:]),
                after=([t_cz] if c == 0 else [cum_last[g] for g in range(G)]))
            snap_w.append(t_snap)
            t_zx = sc.dma("sync", lambda g, c=c: g.dma_start(
                out=zx[:].rearrange("p (s w) -> p s w", s=NSL_XBC),
                in_=z_dram[NSL_GATE:NSL_IP, c * P:(c + 1) * P, :].rearrange("s r w -> r s w")),
                after=m1f + pp_fin + ([prev_zx] if prev_zx else []))
            carry_q = []
            for q4 in range(4):
                if c > 0:
                    carry_q.append(sc.op("dve", lambda v, q4=q4: v.tensor_copy(
                        xbct_v[:, q4 * 20:(q4 + 1) * 20, 0:3],
                        xbct_v[:, q4 * 20:(q4 + 1) * 20, 128:131]),
                        after=[prev_sil_q[q4]] + ([carry_q[-1]] if carry_q else [])))
                else:
                    carry_q.append(None)
            carry_dep0 = ([t_carry[-1]] + pp_fin) if c == 0 else []
            ptr_last = [None, None]
            tr_evs = []
            for g4 in range(20):
                par = g4 % 2
                q4h = g4 // 5
                trl = None
                for k4 in range(4):
                    m = 4 * g4 + k4
                    af3 = [t_zx, t_idn] + carry_dep0 + \
                          ([carry_q[q4h]] if carry_q[q4h] else []) + \
                          ([ptr_last[par]] if (k4 == 0 and ptr_last[par]) else []) + \
                          ([trl] if trl else [])
                    trl = sc.op("pe", lambda t_, m=m, k4=k4, par=par: t_.transpose(
                        p_tr[par][:, k4 * P:(k4 + 1) * P],
                        zx[:, (m // 4) * NW + (m % 4) * P:(m // 4) * NW + (m % 4 + 1) * P],
                        idn[:]), after=af3)
                af3e = [trl] + carry_dep0 + \
                       ([carry_q[q4h]] if carry_q[q4h] else []) + \
                       ([prev_sil_q[q4h]] if (c > 0 and prev_sil_q[q4h]) else [])
                ev = sc.op("act", lambda s, g4=g4, par=par: s.copy(
                    xbct_v[:, 4 * g4:4 * g4 + 4, 3:131],
                    p_tr[par][:].rearrange("p (m q) -> p m q", q=P)), after=af3e)
                ptr_last[par] = ev
                tr_evs.append(ev)
            prev_zx = tr_evs[-1]
            sil_done = []
            q_sil = {}
            prev_cts = {"dve": [None, None], "pool": [None, None]}
            for m in range(NXB):
                eng = "dve" if (m % 3 != 1) else "pool"
                cslot = (m // 2) % 2
                ct = ctm[0 if eng == "dve" else 1][cslot]
                base = m * 131
                af = [tr_evs[m // 4]] + ([carry_q[m // 20]] if carry_q[m // 20] else carry_dep0)
                pc = prev_cts[eng][cslot]
                if eng == "dve":
                    c1 = sc.op(eng, lambda v, m=m, base=base, ct=ct: v.tensor_scalar(
                        ct[:], xbct[:, base:base + P], convw[:, 4 * m:4 * m + 1],
                        None, ALU.mult), after=af + ([pc] if pc else []))
                else:
                    c1 = sc.op(eng, lambda v, m=m, base=base, ct=ct: v.tensor_tensor(
                        ct[:].rearrange("p (o q) -> p o q", o=1),
                        xbct[:, base:base + P].rearrange("p (o q) -> p o q", o=1),
                        convw[:, 4 * m:4 * m + 1].to_broadcast([P, 1, P]),
                        ALU.mult), after=af + ([pc] if pc else []))
                if eng == "dve":
                    for tp in range(1, 4):
                        c1 = sc.op(eng, lambda v, m=m, tp=tp, base=base, ct=ct: v.scalar_tensor_tensor(
                            ct[:], xbct[:, base + tp:base + tp + P],
                            convw[:, 4 * m + tp:4 * m + tp + 1], ct[:],
                            ALU.mult, ALU.add), after=[c1])
                else:
                    ct2 = ctm[1][(cslot + 1) % 2] if False else ctp2
                    for tp in range(1, 4):
                        c1a = sc.op(eng, lambda v, m=m, tp=tp, base=base, ct2=ct2: v.tensor_tensor(
                            ct2[:].rearrange("p (o q) -> p o q", o=1),
                            xbct[:, base + tp:base + tp + P].rearrange("p (o q) -> p o q", o=1),
                            convw[:, 4 * m + tp:4 * m + tp + 1].to_broadcast([P, 1, P]),
                            ALU.mult), after=[c1])
                        c1 = sc.op(eng, lambda v, ct=ct, ct2=ct2: v.tensor_tensor(
                            ct[:], ct[:], ct2[:], ALU.add), after=[c1a])
                dst_off = (m * P) if m < 64 else (c * 2048 + (m - 64) * P)
                dst_t = xbcp if m < 64 else BCT
                sil = sc.op("act", lambda s, m=m, dst_t=dst_t, dst_off=dst_off, ct=ct: s.activation(
                    dst_t[:, dst_off:dst_off + P], ct[:], AF.Silu,
                    bias=convb[:, m:m + 1], scale=1.0),
                    after=[c1] + ([prev_xtr_q[m // 20]] if prev_xtr_q[m // 20] else []) +
                          ([q_sil.get(m - 1)] if m >= 1 and q_sil.get(m - 1) else []))
                q_sil[m] = sil
                prev_cts[eng][cslot] = sil
                sil_done.append(sil)
            sil_q4 = [sil_done[19], sil_done[39], sil_done[59], sil_done[79]]
            for q4 in range(4):
                prev_sil_q[q4] = sil_q4[q4]
            xe_last = [None, None]
            xevs = []
            for g4 in range(16):
                par = g4 % 2
                trl = None
                for k4 in range(4):
                    m = 4 * g4 + k4
                    af4 = [q_sil[m], t_idn] + \
                          ([xe_last[par]] if (k4 == 0 and xe_last[par]) else []) + \
                          ([trl] if trl else []) + \
                          ([tr_evs[-1]] if (g4 <= 1 and k4 == 0) else [])
                    trl = sc.op("pe", lambda t_, m=m, k4=k4, par=par: t_.transpose(
                        p_tr[par][:, k4 * P:(k4 + 1) * P],
                        xbcp[:, m * P:(m + 1) * P], idn[:]), after=af4)
                ev = sc.op("act", lambda s, g4=g4, par=par: s.copy(
                    xtok[:, g4 * 4 * P:(g4 + 1) * 4 * P], p_tr[par][:]),
                    after=[trl] + ([prev_xt_dma] if (g4 == 0 and prev_xt_dma) else []))
                xe_last[par] = ev
                xevs.append(ev)
            t_xw = sc.dma("sync", lambda g, c=c: g.dma_start(
                out=xt_dram[c], in_=xtok[:]), after=xevs)
            prev_xt_dma = t_xw
            for q4 in range(4):
                prev_xtr_q[q4] = xevs[min(5 * q4 + 4, 15)]
            for g_i in range(G):
                afb = [q_sil[64 + g_i], q_sil[72 + g_i], t_idn] + \
                      ([xe_last[g_i % 2]] if xe_last[g_i % 2] else [])
                trb = sc.op("pe", lambda t_, c=c, g_i=g_i: t_.transpose(
                    p_tr[g_i % 2][:, 0:P], BCT[:, c * 2048 + g_i * P:c * 2048 + (g_i + 1) * P],
                    idn[:]), after=afb)
                trbe = sc.op("act", lambda s: s.copy(btokc[:], p_tr[g_i % 2][:, 0:P]),
                             after=[trb] + ([prev_stg] if prev_stg else []))
                xe_last[g_i % 2] = trbe
                x2 = sc.op("dve", lambda v, c=c, g_i=g_i: v.tensor_tensor(
                    xs2g[:].rearrange("p (h e) -> p h e", h=16),
                    xtok[:, g_i * 1024:(g_i + 1) * 1024].rearrange("p (h e) -> p h e", h=16),
                    dsx[:, c * P + g_i * 16:c * P + (g_i + 1) * 16].to_broadcast([P, 16, HD]),
                    ALU.mult),
                    after=[xevs[g_i * 2 + 1], ed] + ([prev_stg] if prev_stg else []))
                mm_last = None
                for hh in range(16):
                    mm_last = sc.op("pe", lambda t_, hh=hh: t_.matmul(
                        p_st[:, hh * HD:(hh + 1) * HD], btokc[:], xs2g[:, hh * HD:(hh + 1) * HD],
                        start=(hh % 8 == 0), stop=(hh % 8 == 7)),
                        after=([trbe, x2] + ([cum_last[g_i]] if (g_i in cum_last) else [])
                               if hh == 0 else []))
                u1 = sc.op("dve", lambda v, c=c, g_i=g_i: v.tensor_tensor(
                    cum[:, g_i * 1024:(g_i + 1) * 1024].rearrange("p (h e) -> p h e", h=16),
                    cum[:, g_i * 1024:(g_i + 1) * 1024].rearrange("p (h e) -> p h e", h=16),
                    lam_all[:, c * P + g_i * 16:c * P + (g_i + 1) * 16].to_broadcast([P, 16, HD]),
                    ALU.mult),
                    after=[ml8[-1], t_cz, snap_w[-1]] + ([cum_last[g_i]] if g_i in cum_last else []))
                u2 = sc.op("dve", lambda v, g_i=g_i: v.tensor_tensor(
                    cum[:, g_i * 1024:(g_i + 1) * 1024],
                    cum[:, g_i * 1024:(g_i + 1) * 1024], p_st[:], ALU.add),
                    after=[u1, mm_last])
                cum_last[g_i] = u2
                prev_stg = u2
            st_final = prev_stg
        fr2a = sc.frontier()
        t_sc = sc.dma("sync", lambda g: g.dma_start(out=cc_st_in[:], in_=cum[:]),
                      after=[cum_last[g] for g in range(G)])
        t_ccs = sc.cc(lambda g: g.collective_compute(
            "AllGather", ALU.bypass, replica_groups=[list(range(NCORES))],
            ins=[cc_st_in[:]], outs=[cc_st_out[:]]), after=[t_sc] + cc_chain)
        cc_chain.append(t_ccs)
        pop_to(mA2)

        mB2 = mark()
        deps_b = deps_m + fr2a + [t_sc]
        xtc = [sb(f"xtc{i}", [P, DIN], BF16) for i in range(2)]
        gatec = sb("gatec", [P, DIN], BF16)
        sinf = sb("sinf", [P, DIN], BF16)
        cumtrue = sb("cumtrue", [P, DIN], BF16)
        ysb = sb("ysb", [P, DIN], F32)
        ytmp = sb("ytmp", [P, NW], F32)
        gml = sb("gml", [P, 2 * 1024], BF16)
        ssb = sb("ssb", [P, 2 * 1024], BF16)
        afla = [sb(f"afla{i}", [1, 2048], BF16) for i in range(4)]
        aflb = [sb(f"aflb{i}", [16, P], BF16) for i in range(4)]
        negid = sb("negid_b", [16, 2048], BF16)
        grst = sb("grst", [P, 16], F32)
        ytst = sb("ytst", [P, 16 * P], BF16)
        ssel = sb("ssel", [P, 1], F32)
        pgm = ps("pgm", [P, P], F32)
        pS = [ps(f"pS{i}", [P, 1024], F32) for i in range(2)]
        pY = ps("pY", [P, 1024], F32)
        p_tr2 = ps("p_tr2", [P, 4 * P], BF16)

        t_ng = sc.dma("pool", lambda g: g.dma_start(out=negid[:], in_=negid_in), after=deps_b)
        t_sg = sc.dma("pool", lambda g: g.indirect_dma_start(
            ysb[:], None, cc_st_out[:], bass.IndirectOffsetOnAxis(sidx[:, 0:1], 0)),
            after=[t_ccs, t_si] + deps_b)
        t_ss = sc.dma("sync", lambda g: g.dma_start(out=ssel[:], in_=st_sel_in), after=deps_b)
        i1 = sc.op("dve", lambda v: v.tensor_scalar(
            sinf[:], ysb[:], ssel[:, 0:1], None, ALU.mult), after=[t_sg, t_ss])

        prev_xt_p = [None, None]
        prev_gate = None
        prev_ytw = None
        prev_ct_use = i1
        prev_sinf = i1
        prev_Yev = i1
        tr2_last = None
        t_af_prev = {}
        flat_prev = {}
        for c in range(NT):
            t_xt = sc.dma("sync", lambda g, c=c: g.dma_start(out=xtc[c % 2][:], in_=xt_dram[c]),
                          after=deps_b + ([prev_xt_p[c % 2]] if prev_xt_p[c % 2] else []))
            t_gt = sc.dma("sync", lambda g, c=c: g.dma_start(
                out=gatec[:].rearrange("p (s w) -> p s w", s=NSL_GATE),
                in_=z_dram[0:NSL_GATE, c * P:(c + 1) * P, :].rearrange("s r w -> r s w")),
                after=deps_b + ([prev_gate] if prev_gate else []))
            if c == 0:
                ctk = sc.op("dve", lambda v: v.tensor_copy(cumtrue[:], sinf[:]),
                            after=[i1, prev_ct_use])
            else:
                sl = sc.op("dve", lambda v, c=c: v.tensor_tensor(
                    sinf[:].rearrange("p (h e) -> p h e", h=NH),
                    sinf[:].rearrange("p (h e) -> p h e", h=NH),
                    lam_all[:, (c - 1) * P:c * P].to_broadcast([P, NH, HD]),
                    ALU.mult), after=[prev_sinf])
                prev_sinf = sl
                ld = sc.dma("sync", lambda g, c=c: g.dma_start(
                    out=cumtrue[:], in_=snap_dram[c]),
                    after=[snap_w[c], prev_ct_use] + deps_b)
                ctk = sc.op("dve", lambda v: v.tensor_tensor(
                    cumtrue[:], cumtrue[:], sinf[:], ALU.add), after=[ld, sl])
            t_af = {}

            def load_flats(g_i, extra):
                par_g = g_i % 2
                for hl in range(2):
                    slota = par_g * 2 + hl
                    paf = [x for x in (t_af_prev.get(("f", slota)),
                                       t_af_prev.get(("f2", slota))) if x] + extra
                    t_af[(g_i, "a", hl)] = sc.dma("pool", lambda g, c=c, g_i=g_i, hl=hl, slota=slota:
                        g.dma_start(
                            out=afla[slota][:].rearrange("o (h j) -> o h j", h=16),
                            in_=acst_dram[hl][g_i * 16:(g_i + 1) * 16, c * P:(c + 1) * P]
                            .rearrange("(o h) j -> o h j", o=1)),
                        after=deps_b + paf)
                    t_af[(g_i, "b", hl)] = sc.dma("sync", lambda g, c=c, g_i=g_i, hl=hl, slota=slota:
                        g.dma_start(
                            out=aflb[slota][:],
                            in_=acst_dram[2 + hl][g_i * 16:(g_i + 1) * 16, c * P:(c + 1) * P]),
                        after=deps_b + paf)

            load_flats(0, list(flat_prev.get(0, [])))
            load_flats(1, list(flat_prev.get(1, [])))
            prev_S = [None, None]
            gml_use = [None, None]
            y_evs = []
            for g_i in range(G):
                gm = sc.op("pe", lambda t_, c=c, g_i=g_i: t_.matmul(
                    pgm[:], BCT[:, c * 2048 + g_i * P:c * 2048 + (g_i + 1) * P],
                    BCT[:, c * 2048 + 1024 + g_i * P:c * 2048 + 1024 + (g_i + 1) * P],
                    start=True, stop=True),
                    after=deps_b + [x for x in (gml_use[0], gml_use[1]) if x])
                for b2 in range(2):
                    h0 = g_i * 16 + b2 * 8
                    par = (2 * g_i + b2) % 2
                    pg2 = (g_i % 2) * 2
                    afS = [t_af[(g_i, "a", 0)], t_af[(g_i, "a", 1)],
                           t_af[(g_i, "b", 0)], t_af[(g_i, "b", 1)], t_ones, t_ng, t_m4c] + \
                          [x for x in (prev_S[par], gml_use[par]) if x]
                    r1 = None
                    for nn in range(2):
                        cb = (b2 * 8 + nn * 4) * P
                        r1 = sc.op("pe", lambda t_, pg2=pg2, par=par, cb=cb, nn=nn: t_.matmul(
                            pS[par][:, nn * NW:(nn + 1) * NW], ones_row[0:1, :],
                            afla[pg2][0:1, cb:cb + 4 * P],
                            start=True, stop=False), after=(afS if nn == 0 else [r1]))
                        r1 = sc.op("pe", lambda t_, pg2=pg2, par=par, cb=cb, nn=nn: t_.matmul(
                            pS[par][:, nn * NW:(nn + 1) * NW], ones_row[0:1, :],
                            afla[pg2 + 1][0:1, cb:cb + 4 * P],
                            start=False, stop=False), after=[r1])
                        r1 = sc.op("pe", lambda t_, pg2=pg2, par=par, cb=cb, nn=nn: t_.matmul(
                            pS[par][:, nn * NW:(nn + 1) * NW], aflb[pg2][:, :],
                            negid[:, cb:cb + 4 * P],
                            start=False, stop=False), after=[r1])
                        r1 = sc.op("pe", lambda t_, pg2=pg2, par=par, cb=cb, nn=nn: t_.matmul(
                            pS[par][:, nn * NW:(nn + 1) * NW], aflb[pg2 + 1][:, :],
                            negid[:, cb:cb + 4 * P],
                            start=False, stop=False), after=[r1])
                        r1 = sc.op("pe", lambda t_, par=par, nn=nn: t_.matmul(
                            pS[par][:, nn * NW:(nn + 1) * NW], idn[:], mask4[:, 0:NW],
                            start=False, stop=True), after=[r1, t_idn])
                    sE = sc.op("act", lambda s, par=par: s.activation(
                        ssb[:, par * 1024:(par + 1) * 1024], pS[par][:],
                        AF.Exp, scale=1.0), after=[r1] +
                        ([gml_use[par]] if gml_use[par] else []))
                    prev_S[par] = sE
                    gu = []
                    for hh in range(8):
                        gu.append(sc.op("dve", lambda v, hh=hh, par=par: v.tensor_tensor(
                            gml[:, par * 1024 + hh * P:par * 1024 + (hh + 1) * P],
                            ssb[:, par * 1024 + hh * P:par * 1024 + (hh + 1) * P],
                            pgm[:], ALU.mult),
                            after=[sE, gm] + ([gml_use[par]] if (hh == 0 and gml_use[par]) else [])))
                    ym = None
                    for hh in range(8):
                        ym = sc.op("pe", lambda t_, c=c, hh=hh, par=par, h0=h0: t_.matmul(
                            pY[:, hh * HD:(hh + 1) * HD],
                            gml[:, par * 1024 + hh * P:par * 1024 + (hh + 1) * P],
                            xtc[c % 2][:, (h0 + hh) * HD:(h0 + hh + 1) * HD],
                            start=(hh == 0), stop=(hh == 7)),
                            after=[gu[hh], t_xt] + ([prev_Yev] if (hh == 0 and prev_Yev) else []))
                    for hh in range(8):
                        ym = sc.op("pe", lambda t_, c=c, g_i=g_i, hh=hh, b2=b2, h0=h0: t_.matmul(
                            pY[:, NW + hh * HD:NW + (hh + 1) * HD],
                            BCT[:, c * 2048 + 1024 + g_i * P:c * 2048 + 1024 + (g_i + 1) * P],
                            cumtrue[:, (h0 + hh) * HD:(h0 + hh + 1) * HD],
                            start=(hh == 0), stop=(hh == 7)), after=[ym, ctk])
                    gml_use[par] = ym
                    v1 = sc.op("dve", lambda v, c=c, h0=h0: v.tensor_tensor(
                        ytmp[:].rearrange("p (h e) -> p h e", h=8),
                        pY[:, NW:2 * NW].rearrange("p (h e) -> p h e", h=8),
                        sdv[:, c * P + h0:c * P + h0 + 8].to_broadcast([P, 8, HD]),
                        ALU.mult), after=[ym])
                    v2 = sc.op("dve", lambda v, h0=h0: v.tensor_tensor(
                        ysb[:, h0 * HD:(h0 + 8) * HD], pY[:, 0:NW], ytmp[:], ALU.add),
                        after=[v1])
                    v3 = sc.op("dve", lambda v, c=c, h0=h0: v.tensor_tensor(
                        ytmp[:].rearrange("p (h e) -> p h e", h=8),
                        xtc[c % 2][:, h0 * HD:(h0 + 8) * HD].rearrange("p (h e) -> p h e", h=8),
                        dbc[:, h0:h0 + 8].to_broadcast([P, 8, HD]), ALU.mult),
                        after=[v2, t_xt])
                    v4 = sc.op("dve", lambda v, h0=h0: v.tensor_tensor(
                        ysb[:, h0 * HD:(h0 + 8) * HD], ysb[:, h0 * HD:(h0 + 8) * HD],
                        ytmp[:], ALU.add), after=[v3])
                    prev_Yev = v4
                    y_evs.append(v4)
                for _hl in range(2):
                    t_af_prev[("f", (g_i % 2) * 2 + _hl)] = prev_S[0]
                    t_af_prev[("f2", (g_i % 2) * 2 + _hl)] = prev_S[1]
                nxt = [x for x in prev_S if x]
                if g_i + 2 < G:
                    load_flats(g_i + 2, nxt)
                else:
                    flat_prev[g_i - 6] = nxt
            prev_ct_use = y_evs[-1]
            sg = sc.op("act", lambda s: s.activation(gatec[:], gatec[:], AF.Silu, scale=1.0),
                       after=[t_gt])
            gm2 = sc.op("dve", lambda v: v.tensor_tensor(ysb[:], ysb[:], gatec[:], ALU.mult),
                        after=[sg] + y_evs[-2:])
            nsq = []
            for g_i in range(G):
                nsq.append(sc.op("act", lambda s, g_i=g_i: s.activation(
                    gatec[:, g_i * 1024:(g_i + 1) * 1024], ysb[:, g_i * 1024:(g_i + 1) * 1024],
                    AF.Square, accum_out=grst[:, g_i:g_i + 1]), after=[gm2, sg]))
            n1 = sc.op("act", lambda s: s.activation(
                grst[:, 8:16], grst[:, 0:8], AF.Ln, bias=EPS, scale=1.0 / 1024), after=nsq)
            n2 = sc.op("act", lambda s: s.activation(
                grst[:, 8:16], grst[:, 8:16], AF.Exp, scale=-0.5), after=[n1])
            napp = []
            for g_i in range(G):
                napp.append(sc.op("dve", lambda v, g_i=g_i: v.tensor_scalar(
                    gatec[:, g_i * 1024:(g_i + 1) * 1024], ysb[:, g_i * 1024:(g_i + 1) * 1024],
                    grst[:, 8 + g_i:9 + g_i], None, ALU.mult),
                    after=[n2, nsq[g_i]]))
            prev_gate = napp[-1]
            yb = napp[-1]
            prev_xt_p[c % 2] = y_evs[-1]
            for half in range(4):
                yws = []
                for q4 in range(4):
                    trl = None
                    for k4 in range(4):
                        m = half * 16 + q4 * 4 + k4
                        trl = sc.op("pe", lambda t_, m=m, k4=k4: t_.transpose(
                            p_tr2[:, k4 * P:(k4 + 1) * P], gatec[:, m * P:(m + 1) * P], idn[:]),
                            after=[napp[m // 8], t_idn] + ([tr2_last] if (k4 == 0 and tr2_last) else []) +
                                  ([trl] if trl else []))
                    m0q = half * 16 + q4 * 4
                    ye = sc.op("dve", lambda v, q4=q4, m0q=m0q: v.tensor_tensor(
                        ytst[:, q4 * 4 * P:(q4 + 1) * 4 * P].rearrange("p (m q) -> p m q", q=P),
                        p_tr2[:].rearrange("p (m q) -> p m q", q=P),
                        nwc[:, m0q:m0q + 4].to_broadcast([P, 4, P]), ALU.mult),
                        after=[trl, t_nw] + ([prev_ytw] if (q4 == 0 and prev_ytw) else []))
                    yws.append(ye)
                    tr2_last = ye
                prev_ytw = sc.dma("sync", lambda g, c=c, half=half: g.dma_start(
                    out=yt_dram[c][half * 16:(half + 1) * 16].rearrange("m p q -> p m q"),
                    in_=ytst[:].rearrange("p (m q) -> p m q", m=16)), after=yws)
        fr2b = sc.frontier()
        pop_to(mS)

        mC2 = mark()
        deps_c = deps + fr2b
        ytf = sb("ytf", [P, 64 * TOK], BF16)
        wob = [sb(f"wobm{i}", [P, 64 * P], BF16) for i in range(2)]
        obf = sb("obf", [P, TOK], BF16)
        stf = [sb(f"stf{i}", [P, NW], F32) for i in range(4)]
        pb_ = [ps(f"mpb{i}", [P, NW], F32) for i in range(4)]
        p_tr3 = [ps(f"p_tr3{i}", [P, NW], BF16) for i in range(2)]
        if dst_rm:
            t_init = sc.dma("sync", lambda g: g.dma_start(
                out=dst_dram,
                in_=(src_dram if src_rm else
                     src_dram.rearrange("t s p w -> (t p) (s w)"))), after=deps)
        else:
            t_init = None
            for tt_ in range(NT):
                t_init = sc.dma("sync", lambda g, tt_=tt_: g.dma_start(
                    out=dst_dram[tt_],
                    in_=(src_dram[tt_ * P:(tt_ + 1) * P, :].rearrange(
                        "p (s w) -> s p w", w=NW) if src_rm else src_dram[tt_])),
                    after=deps)
        t_ytf_l = [sc.dma(("sync", "pool", "act")[c % 3], lambda g, c=c: g.dma_start(
            out=ytf[:].rearrange("p (m cq) -> p m cq", m=64)[:, :, c * P:(c + 1) * P],
            in_=yt_dram[c].rearrange("m p q -> p m q")), after=deps_c) for c in range(NT)]
        t_ytf = t_ytf_l[-1]
        buf_use3 = [None, None]
        prev_stf = [None, None]
        stf_use = [None, None, None, None]
        prev_obf = None
        for m in range(32):
            af = list(deps_c)
            if buf_use3[m % 2] is not None:
                af.append(buf_use3[m % 2])
            t_w = sc.dma("pool", lambda g, m=m: g.dma_start(out=wob[m % 2][:], in_=w_ob[m]),
                         after=af)
            mml = None
            for kt in range(64):
                for ns2 in range(2):
                    af2 = []
                    if kt == 0 and ns2 == 0:
                        af2 = [t_w] + t_ytf_l + ([prev_obf] if prev_obf else [])
                    mml = sc.op("pe", lambda t_, m=m, kt=kt, ns2=ns2: t_.matmul(
                        pb_[2 * (m % 2) + ns2][:], wob[m % 2][:, kt * P:(kt + 1) * P],
                        ytf[:, kt * TOK + ns2 * NW:kt * TOK + (ns2 + 1) * NW],
                        start=(kt == 0), stop=(kt == 63)), after=af2)
            buf_use3[m % 2] = mml
            ev1 = sc.op("act", lambda s, m=m: s.copy(obf[:, 0:NW], pb_[2 * (m % 2)][:]),
                        after=[mml] + ([prev_obf] if prev_obf else []))
            ev2 = sc.op("act", lambda s, m=m: s.copy(obf[:, NW:TOK], pb_[2 * (m % 2) + 1][:]),
                        after=[mml, ev1])
            obf_reads = []
            for half in range(2):
                sidx4 = (2 * m + half) % 4
                trl = None
                for q4 in range(4):
                    t_t = half * 4 + q4
                    trl = sc.op("pe", lambda t_, t_t=t_t, q4=q4, half=half: t_.transpose(
                        p_tr3[half][:, q4 * P:(q4 + 1) * P], obf[:, t_t * P:(t_t + 1) * P],
                        idn[:]),
                        after=[ev1, ev2, t_idn] +
                              ([prev_stf[half]] if (q4 == 0 and prev_stf[half]) else []) +
                              ([trl] if trl else []))
                sv = sc.op("dve", lambda v, half=half, sidx4=sidx4: v.tensor_copy(
                    stf[sidx4][:], p_tr3[half][:]),
                    after=[trl] + ([stf_use[sidx4]] if stf_use[sidx4] else []))
                obf_reads.append(trl)
                if dst_rm:
                    wdma = sc.dma("pool", lambda g, m=m, half=half, sidx4=sidx4: g.dma_start(
                        out=dst_dram.rearrange("(t p) f -> p t f", p=P)[
                            :, half * 4:(half + 1) * 4, m * P:(m + 1) * P],
                        in_=stf[sidx4][:].rearrange("p (t q) -> p t q", t=4),
                        accum_op=ALU.add), after=[sv, t_init])
                else:
                    wdma = sc.dma("pool", lambda g, m=m, half=half, sidx4=sidx4: g.dma_start(
                        out=dst_dram[half * 4:(half + 1) * 4, m // 4, :,
                                     (m % 4) * P:(m % 4 + 1) * P].rearrange("t p q -> p t q"),
                        in_=stf[sidx4][:].rearrange("p (t q) -> p t q", t=4),
                        accum_op=ALU.add), after=[sv, t_init])
                prev_stf[half] = sv
                stf_use[sidx4] = wdma
            prev_obf = obf_reads[-1]
        fr = sc.frontier()
        pop_to(m0)
        return fr

    def emit_attn(src_dram, src_rm, dst_dram, dst_rm, deps):
        m0 = mark()
        qT = sb("qT", [P, 8 * TOK], BF16)
        aot = sb("aot", [P, 8 * TOK], BF16)
        bo_row = sb("bo_row", [1, D], BF16)
        bqkv_row = sb("bqkv_row", [1, 3072], BF16)
        rcp = sb("rcp", [P, NT], F32)
        mA = mark()
        xnT = sb("xnT_a", [P, KT * TOK], BF16)
        mL = mark()
        ln_bufs = mk_ln_bufs("aln")
        xn_done, t_norm = emit_ln(src_dram, src_rm, ln_bufs, xnT, deps)
        pop_to(mL)
        deps = list(deps) + [xn_done[-1], t_norm[-1]]

        t_bo = sc.dma("pool", lambda g: g.dma_start(out=bo_row[:], in_=bo_in), after=deps)
        t_bq = sc.dma("pool", lambda g: g.dma_start(out=bqkv_row[:], in_=bqkv_in), after=deps)

        ktb = sb("ktb", [P, 8 * TOK], BF16)
        vst = sb("vst", [P, 8 * TOK], BF16)
        qtmp = sb("qtmp", [P, NW], BF16)
        wqb = [sb(f"wq{i}", [P, 16 * NW], BF16) for i in range(3)]
        pqk = [ps(f"pqk{i}", [P, NW], F32) for i in range(2)]
        ptq = [ps(f"ptq{i}", [P, P], BF16) for i in range(2)]
        buf_use = [None, None, None]
        prev_tr = [None, None]
        prev_tmp = None
        prev_ev = {}
        tr_done = []
        for s_i in range(6):
            t_wh = [None, None]
            for hf in range(2):
                bidx = (2 * s_i + hf) % 3
                af = list(deps)
                if buf_use[bidx] is not None:
                    af.append(buf_use[bidx])
                t_wh[hf] = sc.dma("pool", lambda g, s_i=s_i, hf=hf, bidx=bidx: g.dma_start(
                    out=wqb[bidx][:], in_=w_qkv[s_i][:, hf * 16 * NW:(hf + 1) * 16 * NW]),
                    after=af)
            for t in range(NT):
                gi = s_i * NT + t
                pb = pqk[gi % 2]
                af2 = [t_wh[0]] + deps
                if gi >= 2:
                    af2.append(prev_ev[gi % 2])
                mmlast = None
                for kt in range(KT):
                    hf = kt // 16
                    bidx = (2 * s_i + hf) % 3
                    mmlast = sc.op("pe", lambda t_, t=t, kt=kt, pb=pb, bidx=bidx: t_.matmul(
                        pb[:], xnT[:, kt * TOK + t * P:kt * TOK + (t + 1) * P],
                        wqb[bidx][:, (kt % 16) * NW:(kt % 16 + 1) * NW],
                        start=(kt == 0), stop=False),
                        after=(af2 if kt == 0 else ([t_wh[1]] if kt == 16 else [])))
                    if t == NT - 1 and kt % 16 == 15:
                        buf_use[bidx] = mmlast
                mb = sc.op("pe", lambda t_, s_i=s_i, pb=pb: t_.matmul(
                    pb[:], ones_row[0:1, :], bqkv_row[0:1, s_i * NW:(s_i + 1) * NW],
                    start=False, stop=True), after=[mmlast, t_bq, t_ones])
                if s_i < 4:
                    scl = 0.03125 if s_i < 2 else 1.0
                    ev = sc.op("dve", lambda v, pb=pb, scl=scl: v.tensor_scalar(
                        qtmp[:], pb[:], scl, None, ALU.mult),
                        after=[mb] + ([prev_tmp] if prev_tmp else []))
                    trl = []
                    for q4 in range(4):
                        dt = (s_i % 2) * 4 + q4
                        af3 = [ev, t_idn] + ([prev_tr[q4 % 2]] if prev_tr[q4 % 2] else [])
                        tr = sc.op("pe", lambda t_, q4=q4: t_.transpose(
                            ptq[q4 % 2][:], qtmp[:, q4 * P:(q4 + 1) * P], idn[:]), after=af3)
                        dst_sb = qT if s_i < 2 else ktb
                        tev = sc.op("act", lambda s, dt=dt, t=t, q4=q4, dst_sb=dst_sb: s.copy(
                            dst_sb[:, dt * TOK + t * P:dt * TOK + (t + 1) * P],
                            ptq[q4 % 2][:]), after=[tr])
                        prev_tr[q4 % 2] = tev
                        trl.append(tev)
                    prev_tmp = trl[-1]
                    prev_ev[gi % 2] = trl[-1]
                    tr_done.extend(trl)
                else:
                    ev = sc.op("act", lambda s, s_i=s_i, t=t, pb=pb: s.copy(
                        vst[:, t * TOK + (s_i - 4) * NW: t * TOK + (s_i - 3) * NW], pb[:]),
                        after=[mb])
                    prev_ev[gi % 2] = ev
                    tr_done.append(ev)

        t_c1 = sc.dma("sync", lambda g: g.dma_start(
            out=cc_kv_in[0:1024].rearrange("(dt p) k -> p dt k", p=P),
            in_=ktb[:].rearrange("p (dt k) -> p dt k", dt=8)), after=tr_done)
        t_c2 = sc.dma("sync", lambda g: g.dma_start(
            out=cc_kv_in[1024:2048].rearrange("(t p) d -> p t d", p=P),
            in_=vst[:].rearrange("p (t d) -> p t d", t=8)), after=tr_done)
        af_cc = [t_c1, t_c2] + cc_chain
        t_cc = sc.cc(lambda g: g.collective_compute(
            "AllGather", ALU.bypass, replica_groups=[list(range(NCORES))],
            ins=[cc_kv_in[:]], outs=[cc_kv_out[:]]), after=af_cc)
        cc_chain.append(t_cc)
        pop_to(mA)

        mB = mark()
        ktf = sb("ktf", [P, 8 * 2048], BF16)
        vf = sb("vf", [P, 16 * TOK], BF16)
        kvidx = sb("kvidx", [P, 32], U32)
        pexp = [sb(f"pexp{i}", [P, 2048], BF16) for i in range(2)]
        ptb = [sb(f"ptb{i}", [P, 16 * P], BF16) for i in range(2)]
        avb = [sb(f"avb{i}", [P, TOK], BF16) for i in range(2)]
        smax = sb("smax", [P, 2 * NT], F32)
        psc = ps("psc", [P, 2048], F32)
        pav = ps("pav", [P, TOK], F32)
        ptr3 = [ps(f"ptr3{i}", [P, P], BF16) for i in range(2)]
        t_ki = sc.dma("sync", lambda g: g.dma_start(out=kvidx[:], in_=kv_idx_in),
                      after=deps + [t_c1, t_c2])
        t_g = []
        for j in range(16):
            dt, hf = j % 8, j // 8
            t_g.append(sc.dma("pool", lambda g, dt=dt, hf=hf, j=j: g.indirect_dma_start(
                ktf[:, dt * 2048 + hf * 1024: dt * 2048 + (hf + 1) * 1024], None,
                cc_kv_out[:], bass.IndirectOffsetOnAxis(kvidx[:, j:j + 1], 0)),
                after=[t_cc, t_ki]))
        for j in range(16):
            t_g.append(sc.dma("pool", lambda g, j=j: g.indirect_dma_start(
                vf[:, j * TOK:(j + 1) * TOK], None,
                cc_kv_out[:], bass.IndirectOffsetOnAxis(kvidx[:, 16 + j:17 + j], 0)),
                after=[t_cc, t_ki]))
        prev_exp = None
        prev_pt = [None, None]
        av_read = [None, None]
        prev_x4 = None
        t_aot = []
        prev_aot = [None, None]
        for t in range(NT):
            af = t_g + ([prev_exp] if prev_exp else [])
            mml = None
            for ns in range(4):
                for dt in range(8):
                    mml = sc.op("pe", lambda t_, t=t, ns=ns, dt=dt: t_.matmul(
                        psc[:, ns * NW:(ns + 1) * NW],
                        qT[:, dt * TOK + t * P:dt * TOK + (t + 1) * P],
                        ktf[:, dt * 2048 + ns * NW: dt * 2048 + (ns + 1) * NW],
                        start=(dt == 0), stop=(dt == 7)),
                        after=(af if (ns == 0 and dt == 0) else []))
            sm = smax[:, 2 * t:2 * t + 2]
            x1 = sc.op("dve", lambda v, sm=sm: v.reduce_max(
                sm[:, 0:1], psc[:], AX, negate=True), after=[mml])
            x2 = sc.op("act", lambda s, t=t, sm=sm: s.activation(
                pexp[t % 2][:], psc[:], AF.Exp, bias=sm[:, 0:1], scale=1.0,
                accum_out=sm[:, 1:2]),
                after=[x1] + ([prev_pt[t % 2]] if prev_pt[t % 2] else []))
            prev_exp = x2
            x3 = sc.op("dve", lambda v, t=t, sm=sm: v.reciprocal(
                rcp[:, t:t + 1], sm[:, 1:2]), after=[x2])
            ptl = []
            for k2 in range(16):
                af4 = [x2, t_idn] + ([ptl[-2]] if len(ptl) >= 2 else [])
                tr = sc.op("pe", lambda t_, t=t, k2=k2: t_.transpose(
                    ptr3[k2 % 2][:], pexp[t % 2][:, k2 * P:(k2 + 1) * P], idn[:]), after=af4)
                tev = sc.op("act", lambda s, t=t, k2=k2: s.copy(
                    ptb[t % 2][:, k2 * P:(k2 + 1) * P], ptr3[k2 % 2][:]),
                    after=[tr] + ([prev_x4] if (k2 == 0 and prev_x4) else []))
                ptl.append(tev)
            prev_pt[t % 2] = ptl[-1]
            afv = ptl + ([prev_x4] if prev_x4 else [])
            mv = None
            for k2 in range(16):
                for ns in range(2):
                    mv = sc.op("pe", lambda t_, t=t, k2=k2, ns=ns: t_.matmul(
                        pav[:, ns * NW:(ns + 1) * NW],
                        ptb[t % 2][:, k2 * P:(k2 + 1) * P],
                        vf[:, k2 * TOK + ns * NW: k2 * TOK + (ns + 1) * NW],
                        start=(k2 == 0), stop=(k2 == 15)),
                        after=(afv if (k2 == 0 and ns == 0) else []))
            x4 = sc.op("dve", lambda v, t=t: v.tensor_scalar(
                avb[t % 2][:], pav[:], rcp[:, t:t + 1], None, ALU.mult),
                after=[mv, x3] + ([av_read[t % 2]] if av_read[t % 2] else []))
            prev_x4 = x4
            for dt in range(8):
                af5 = [x4, t_idn] + ([prev_aot[dt % 2]] if prev_aot[dt % 2] else [])
                tr = sc.op("pe", lambda t_, t=t, dt=dt: t_.transpose(
                    ptr3[dt % 2][:], avb[t % 2][:, dt * P:(dt + 1) * P], idn[:]), after=af5)
                tev = sc.op("act", lambda s, t=t, dt=dt: s.copy(
                    aot[:, dt * TOK + t * P:dt * TOK + (t + 1) * P], ptr3[dt % 2][:]), after=[tr])
                prev_aot[dt % 2] = tev
                t_aot.append(tev)
            av_read[t % 2] = t_aot[-1]
        pop_to(mB)

        mC = mark()
        deps_c = list(deps) + sc.frontier()
        wob = [sb(f"wo2_{i}", [P, 8 * NW], BF16) for i in range(2)]
        stage = [sb(f"as{i}", [P, NW], F32) for i in range(4)]
        resb = [sb(f"ar{i}", [P, NW], F32) for i in range(4)]
        pmm = [ps(f"apm{i}", [P, NW], F32) for i in range(4)]
        buf_use2 = [None, None]
        prev_stage_use = {}
        for s_i in range(8):
            af = list(deps_c)
            if buf_use2[s_i % 2] is not None:
                af.append(buf_use2[s_i % 2])
            t_w = sc.dma("pool", lambda g, s_i=s_i: g.dma_start(
                out=wob[s_i % 2][:], in_=w_o2[s_i]), after=af)
            mml = None
            for t in range(NT):
                gi = s_i * NT + t
                pb = pmm[gi % 4]
                af2 = [t_w] + t_aot[-1:]
                if gi >= 4:
                    af2.append(prev_stage_use[gi % 4])
                for kt in range(8):
                    mml = sc.op("pe", lambda t_, t=t, kt=kt, pb=pb, s_i=s_i: t_.matmul(
                        pb[:], aot[:, kt * TOK + t * P:kt * TOK + (t + 1) * P],
                        wob[s_i % 2][:, kt * NW:(kt + 1) * NW],
                        start=(kt == 0), stop=False), after=(af2 if kt == 0 else []))
                mb = sc.op("pe", lambda t_, s_i=s_i, pb=pb: t_.matmul(
                    pb[:], ones_row[0:1, :], bo_row[0:1, s_i * NW:(s_i + 1) * NW],
                    start=False, stop=True), after=[mml, t_bo])
                rl = sc.dma("sync", lambda g, s_i=s_i, t=t, gi=gi: g.dma_start(
                    out=resb[gi % 4][:],
                    in_=(src_dram[t * P:(t + 1) * P, s_i * NW:(s_i + 1) * NW] if src_rm
                         else src_dram[t, s_i])),
                    after=(deps_c if gi < 4 else [prev_stage_use[gi % 4]]))
                ev = sc.op("dve", lambda v, gi=gi, pb=pb: v.tensor_tensor(
                    stage[gi % 4][:], pb[:], resb[gi % 4][:], ALU.add), after=[mb, rl])
                st = sc.dma("sync", lambda g, s_i=s_i, t=t, gi=gi: g.dma_start(
                    out=(dst_dram[t * P:(t + 1) * P, s_i * NW:(s_i + 1) * NW] if dst_rm
                         else dst_dram[t, s_i]), in_=stage[gi % 4][:]), after=[ev])
                prev_stage_use[gi % 4] = st
            buf_use2[s_i % 2] = mml
        fr = sc.frontier()
        pop_to(m0)
        return fr

    cc_chain = []
    deps = [t_idn, t_idnf, t_ones, t_mones, t_c0, t_c1]
    cur_src = x_ext
    cur_rm = True
    for i, name in enumerate(seq):
        last = (i == len(seq) - 1)
        dst = out_ext if last else streams_d[i]
        dst_rm = last
        if name in ("ffn", "ffn2"):
            deps = emit_ffn(cur_src, cur_rm, dst, dst_rm, deps)
        elif name == "attn":
            deps = emit_attn(cur_src, cur_rm, dst, dst_rm, deps)
        elif name == "mamba":
            deps = emit_mamba(cur_src, cur_rm, dst, dst_rm, deps)
        cur_src, cur_rm = dst, dst_rm

    blk = nc.Block()
    block = blk.__enter__()
    sc.finalize(nc, block, ctx_stack)
    blk.__exit__(None, None, None)
    while ctx_stack:
        ctx_stack.pop()[1].__exit__(None, None, None)
    return nc



def _prep(inputs, seq):
    ip = np.ascontiguousarray(inputs["L0_attn_a_in_proj_w"].T)
    arrs = {}
    arrs["w_ip"] = _pack_a(ip[:, :NSL_IP * NW], NSL_IP)
    arrs["w_dt"] = _pack_a(ip[:, NSL_IP * NW:], 1, nw=P)[0]
    wo = np.ascontiguousarray(inputs["L0_attn_a_out_proj_w"].T)
    arrs["w_ob"] = _pack_b(wo, 32)
    arrs["w_base"] = _pack_a(np.ascontiguousarray(inputs["L0_ffn_base_W"].T), 8)
    AsT = np.ascontiguousarray(inputs["L0_ffn_A"].reshape(P, D).T)
    WrT = np.ascontiguousarray(inputs["L0_ffn_router_W"].T)
    arrs["w_ra"] = _pack_a(np.concatenate([AsT, WrT], axis=1), 1, nw=143)[0]
    arrs["w_lb"] = np.ascontiguousarray(inputs["L0_ffn_B"].transpose(0, 2, 1).reshape(P, D))
    arrs["w_qkv"] = _pack_a(np.ascontiguousarray(inputs["L0_attn_b_Wqkv"]), 6)
    arrs["w_o2"] = _pack_a(np.ascontiguousarray(inputs["L0_attn_b_Wo"]), 8)
    arrs["convw"] = np.ascontiguousarray(
        inputs["L0_attn_a_conv_w"].reshape(CONV_DIM, 4).reshape(NXB, P, 4)
        .transpose(1, 0, 2).reshape(P, NXB * 4))
    arrs["convb"] = np.ascontiguousarray(
        inputs["L0_attn_a_conv_b"].reshape(NXB, P).T)
    arrs["alog"] = inputs["L0_attn_a_A_log"].reshape(1, NH)
    arrs["dvec"] = inputs["L0_attn_a_D"].reshape(1, NH)
    arrs["dtb"] = inputs["L0_attn_a_dt_bias"].reshape(1, NH)
    arrs["normw"] = np.ascontiguousarray(inputs["L0_attn_a_norm_w"].reshape(64, P).T)
    arrs["baseb"] = inputs["L0_ffn_base_b"].reshape(1, D)
    arrs["rb"] = inputs["L0_ffn_router_b"].reshape(1, 15)
    arrs["bqkv"] = inputs["L0_attn_b_bqkv"].reshape(1, 3072)
    arrs["bo"] = inputs["L0_attn_b_bo"].reshape(1, D)
    arrs["emap"] = np.ascontiguousarray(inputs["L0_ffn_expert_map"].reshape(15, 1))
    arrs["ident"] = np.eye(P, dtype=np.float32)
    jj, ii = np.meshgrid(np.arange(P), np.arange(P), indexing="ij")
    arrs["ltones"] = (jj <= ii).astype(np.float32)
    arrs["strl"] = (jj > ii).astype(np.float32)
    arrs["maskneg"] = np.where(jj <= ii, 0.0, NEG_BIG).astype(np.float32)
    e2 = np.zeros((8, P), dtype=np.float32)
    for e in range(8):
        e2[e, e * 16:(e + 1) * 16] = 2.0
    arrs["expand2"] = e2
    arrs["iota8"] = np.tile(np.arange(8, dtype=np.float32), (15, 1))
    ng = np.zeros((16, 2048), dtype=np.float32)
    for k in range(16):
        ng[k, k * 128:(k + 1) * 128] = -1.0
    arrs["negid16"] = ng

    x = np.ascontiguousarray(inputs["L0_attn_a_hidden_states"].astype(np.float32))
    in_maps = []
    for c in range(NCORES):
        b, h = c // 2, c % 2
        m = {k: v for k, v in arrs.items()}
        m["x"] = np.ascontiguousarray(x[b, h * TOK:(h + 1) * TOK, :])
        pair = c ^ 1
        m["halo_idx"] = (np.arange(3, dtype=np.uint32) + (c - 1) * 3 if h == 1
                         else np.zeros(3, dtype=np.uint32)).reshape(3, 1).astype(np.uint32)
        m["halo_sel"] = np.full((3, 1), 1.0 if h == 1 else 0.0, dtype=np.float32)
        m["st_idx"] = ((np.arange(P, dtype=np.uint32) + (c - 1) * P) if h == 1
                       else np.arange(P, dtype=np.uint32)).reshape(P, 1).astype(np.uint32)
        m["st_sel"] = np.full((P, 1), 1.0 if h == 1 else 0.0, dtype=np.float32)
        kvi = np.zeros((P, 32), dtype=np.uint32)
        e_core, o_core = 2 * b, 2 * b + 1
        for t in range(8):
            kvi[:, t] = e_core * 2048 + t * P + np.arange(P)
            kvi[:, 8 + t] = o_core * 2048 + t * P + np.arange(P)
            kvi[:, 16 + t] = e_core * 2048 + 1024 + t * P + np.arange(P)
            kvi[:, 24 + t] = o_core * 2048 + 1024 + t * P + np.arange(P)
        m["kv_idx"] = kvi
        in_maps.append(m)
    return in_maps


_SEQ = ("mamba", "ffn", "attn", "ffn2")
_DBG = False


def kernel(**inputs):
    seq = list(_SEQ)
    in_maps = _prep(inputs, seq)
    nc = build(seq)
    res = run_bass_kernel_spmd(nc, in_maps, core_ids=list(range(NCORES)))
    out = np.empty((4, 2048, D), dtype=np.float32)
    for c in range(NCORES):
        b, h = c // 2, c % 2
        out[b, h * TOK:(h + 1) * TOK, :] = res.results[c]["out"]
    return out
